# version 14
# baseline (speedup 1.0000x reference)
import numpy as np
import ml_dtypes
from contextlib import ExitStack

import concourse.bass as bass
import concourse.mybir as mybir
from concourse.bass_utils import run_bass_kernel_spmd

F32 = mybir.dt.float32
BF16 = mybir.dt.bfloat16
U8 = mybir.dt.uint8
U32 = mybir.dt.uint32
Alu = mybir.AluOpType
AF = mybir.ActivationFunctionType
AX = mybir.AxisListType


from concourse import dve_ops as _dvo
from concourse.dve_op import DveOp as _DveOp
from concourse.dve_spec import (Spec as _Spec, Src0 as _S0, Src1 as _S1,
                                C0 as _C0, C1 as _C1, C2 as _C2,
                                Zero as _Z, One as _One, lower as _dve_lower)


def _fp4a_ref(in0, in1, s0, s1, imm2):
    a = in0 * in1
    v = a + a - s0
    return ((v > 0).astype(in0.dtype) + (v > 1)) + (v > s1)


def _fp4b_ref(in0, in1, s0, s1, imm2):
    a = in0 * in1
    v = a + a - s0
    t = (v > s1).astype(in0.dtype)
    return ((v > 0).astype(in0.dtype) + (t + t)) * in1


def _fp4c_ref(in0, in1, s0, s1, imm2):
    a = in0 * in1
    w = a + a
    t3 = (w > s1).astype(in0.dtype)
    s = (w > s0).astype(in0.dtype) + (t3 + t3)
    return (s + s) * in1


def _register_fp4_ops():
    have = {o.name for o in _dvo.OPS}
    _a = _S0 * _S1
    _v = _a + _a - _C0
    bodyA = ((_v > _Z) + (_v > _One)) + (_v > _C1)
    _t = _v > _C1
    bodyB = ((_v > _Z) + (_t + _t)) * _S1
    _w = _a + _a
    _t3 = _w > _C1
    _s = (_w > _C0) + (_t3 + _t3)
    bodyC = (_s + _s) * _S1
    defs = [("FP4A_ANT", bodyA, _fp4a_ref), ("FP4B_ANT", bodyB, _fp4b_ref),
            ("FP4C_ANT", bodyC, _fp4c_ref)]
    out = []
    for name, body, ref in defs:
        if name in have:
            out.append(next(o for o in _dvo.OPS if o.name == name))
            continue
        sha = {"v3": _dve_lower(_Spec(body=body), ver="v3", perf_en=False,
                                name=name).sha("v3")}
        op = _DveOp(name, _Spec(body=body), False, sha, reference=ref)
        if len(_dvo.OPS) < 31:
            _dvo.OPS.append(op)
            row = _dvo._CUSTOM_DVE_ROW_BASE + len(_dvo.OPS) - 1
        else:
            victim = next(i for i, o in enumerate(_dvo.OPS)
                          if o.name.startswith("DVEHASH"))
            _dvo.CUSTOM_DVE_SPECS.pop(_dvo.OPS[victim].name, None)
            _dvo.CUSTOM_DVE_REFS.pop(_dvo.OPS[victim].name, None)
            _dvo._SUB_OPCODE_FOR_NAME.pop(_dvo.OPS[victim].name, None)
            _dvo.OPS[victim] = op
            row = _dvo._CUSTOM_DVE_ROW_BASE + victim
        _dvo.CUSTOM_DVE_SPECS[name] = op.spec
        _dvo.CUSTOM_DVE_REFS[name] = op.ref
        _dvo._SUB_OPCODE_FOR_NAME[name] = row
        out.append(op)
    return out


FP4A_OP, FP4B_OP, FP4C_OP = _register_fp4_ops()

NCORES = 8
B_FULL, S, D = 16, 2048, 1024
T = (B_FULL // NCORES) * S
TCH = T // 128
DCH = D // 128
I_FFN = 2816
ICH = I_FFN // 128
NHEAD, DHEAD = 8, 64
VD = NHEAD * DHEAD
RSQRT2 = float(2.0 ** -0.5)
LN_EPS = 1e-5
EXP_SHIFT = -8.0


class Ctr:
    def __init__(self, sem):
        self.sem = sem
        self.n = 0

    def inc(self, instr, k=1):
        instr.then_inc(self.sem, k)
        self.n += k
        return self.n

    def mark(self, e):
        e.sem_inc(self.sem, 1)
        self.n += 1
        return self.n


class Arenas:
    def __init__(self, tensors):
        self.tensors = tensors
        self.off = [0] * len(tensors)

    def reset(self):
        self.off = [0] * len(self.tensors)

    def take(self, ncols):
        for i, (t, cap) in enumerate(self.tensors):
            if self.off[i] + ncols <= cap:
                o = self.off[i]
                self.off[i] += ncols
                return t[:, o:o + ncols]
        raise RuntimeError(f"arena overflow: need {ncols}, offs={self.off}")


def build_program():
    nc = bass.Bass("TRN2", target_bir_lowering=False)

    x_in = nc.dram_tensor("x_in", [T, D], F32, kind="ExternalInput").ap()
    Wqk_h = nc.dram_tensor("Wqk", [D, D], F32, kind="ExternalInput").ap()
    bqk_h = nc.dram_tensor("bqk", [D], F32, kind="ExternalInput").ap()
    Wv_h = nc.dram_tensor("Wv", [D, VD], F32, kind="ExternalInput").ap()
    bv_h = nc.dram_tensor("bv", [VD], F32, kind="ExternalInput").ap()
    Wout_h = nc.dram_tensor("Wout", [VD, D], F32, kind="ExternalInput").ap()
    bout_h = nc.dram_tensor("bout", [D], F32, kind="ExternalInput").ap()
    lng_h = nc.dram_tensor("lng", [D], F32, kind="ExternalInput").ap()
    lnb_h = nc.dram_tensor("lnb", [D], F32, kind="ExternalInput").ap()
    Wg_h = nc.dram_tensor("Wg", [I_FFN, D], F32, kind="ExternalInput").ap()
    Wu_h = nc.dram_tensor("Wu", [I_FFN, D], F32, kind="ExternalInput").ap()
    Wd_h = nc.dram_tensor("Wd", [D, I_FFN], F32, kind="ExternalInput").ap()
    Wq1_h = nc.dram_tensor("Wq1", [D, D], F32, kind="ExternalInput").ap()
    W18_h = nc.dram_tensor("W18", [D, D], U8, kind="ExternalInput").ap()
    Wsc_h = nc.dram_tensor("Wsc", [1, 1], F32, kind="ExternalInput").ap()
    b1_h = nc.dram_tensor("b1v", [D], F32, kind="ExternalInput").ap()
    out_h = nc.dram_tensor("out", [T, D], F32, kind="ExternalOutput").ap()

    scrXN0tm = nc.dram_tensor("scrXN0tm", [T, D], F32).ap()
    scrXN0fm = nc.dram_tensor("scrXN0fm", [D, T], F32).ap()
    scrQXfm = nc.dram_tensor("scrQXfm", [D, T], F32).ap()
    scrQWG = nc.dram_tensor("scrQWG", [D, I_FFN], F32).ap()
    scrQWU = nc.dram_tensor("scrQWU", [D, I_FFN], F32).ap()
    scrQWD = nc.dram_tensor("scrQWD", [I_FFN, D], F32).ap()
    scrQH = nc.dram_tensor("scrQH", [I_FFN, T], F32).ap()
    scrG0 = nc.dram_tensor("scrG0", [T, D], F32).ap()
    scrQ = nc.dram_tensor("scrQ", [VD, T], F32).ap()
    scrK = nc.dram_tensor("scrK", [VD, T], F32).ap()
    scrV = nc.dram_tensor("scrV", [T, NHEAD * 65], F32).ap()
    scrO = nc.dram_tensor("scrO", [T, 8 * 65], F32).ap()
    scrX1 = nc.dram_tensor("scrX1", [T, D], F32).ap()
    scrXN1tm = nc.dram_tensor("scrXN1tm", [T, D], F32).ap()
    scrXN1fm = nc.dram_tensor("scrXN1fm", [D, T], F32).ap()
    scrQI = nc.dram_tensor("scrQI", [TCH, D, 128], BF16).ap()
    scrQ1b = nc.dram_tensor("scrQ1b", [TCH, D, 128], F32).ap()
    scrWQT = nc.dram_tensor("scrWQT", [D, D], F32).ap()
    scrWQ2 = nc.dram_tensor("scrWQ2", [D, VD], F32).ap()
    scrWK2 = nc.dram_tensor("scrWK2", [D, VD], F32).ap()
    scrW1T = nc.dram_tensor("scrW1T", [D, D], BF16).ap()
    scrG1 = nc.dram_tensor("scrG1", [2 * D, D], F32).ap()

    es = ExitStack()

    def sbt(name, shape, dt=F32):
        return es.enter_context(nc.sbuf_tensor(name, shape, dt))

    ident = sbt("ident", [128, 128])
    identb = sbt("identb", [128, 128], BF16)
    ones1 = sbt("ones1", [1, 128])
    rows = sbt("rows", [1, D])
    bvrow = sbt("bvrow", [1, VD])
    bc_bout = sbt("bc_bout", [128, D])
    bc_g = sbt("bc_g", [128, D])
    bc_b = sbt("bc_b", [128, D])
    bc_b1 = sbt("bc_b1", [128, D])
    bq_sb = sbt("bq_sb", [128, 4])
    bk_sb = sbt("bk_sb", [128, 4])
    ws_sb = sbt("ws_sb", [128, 1])
    xs_all = sbt("xs_all", [128, TCH])
    stat = sbt("stat", [128, 32])
    cm8 = sbt("cm8", [128, 1])

    a1 = sbt("a1", [128, 18432])
    a2 = sbt("a2", [128, 10240])
    a3 = sbt("a3", [128, 6144])
    a4 = sbt("a4", [128, 4096])
    ab_t = sbt("ab", [128, 8192], BF16)
    u32a = sbt("u32a", [128, 1024], U32)
    u32b = sbt("u32b", [128, 1024], U32)
    u8a = sbt("u8a", [128, 1024], U8)
    AR = Arenas([(a1, 18432), (a2, 10240), (a3, 6144), (a4, 4096)])
    ARB = Arenas([(ab_t, 8192)])

    pbank = [es.enter_context(nc.psum_tensor(f"pb{i}", [128, 512], F32))
             for i in range(8)]

    sem_pool = [Ctr(es.enter_context(nc.semaphore(f"sp{i}"))) for i in range(88)]
    sem_i = [0]

    def fresh():
        c = sem_pool[sem_i[0]]
        sem_i[0] += 1
        return c

    ready = {}
    all_store_ctrs = []

    def wait_stores(e):
        for c in all_store_ctrs:
            if c.n > 0:
                e.wait_ge(c.sem, c.n)

    def wait_ready(e, *names):
        for nm in names:
            if nm in ready:
                c, v = ready[nm]
                if v > 0:
                    e.wait_ge(c.sem, v)

    def quant_fp4_scales(e, src, m_sb, r_sb, W):
        nb = W // 16
        src3 = src[:, 0:W].rearrange("p (b k) -> p b k", k=16)
        m3 = m_sb.rearrange("p (b o) -> p b o", o=1)
        e.tensor_reduce(m3, src3, axis=AX.X, op=Alu.max, apply_absolute_value=True)
        e.drain()
        e.tensor_scalar(m_sb, m_sb, float(np.float32(1.0) / np.float32(6.0)),
                        1e-12, Alu.mult, Alu.max)
        e.drain()
        e.reciprocal(r_sb, m_sb)
        e.drain()
        e.tensor_scalar(m_sb, m_sb, 0.5, None, Alu.mult)
        e.drain()

    def quant_fp4_fast(e, dst, src, u0, bA, bB, bC, m_sb, r_sb, sgnb, W):
        nb = W // 16
        src3 = src[:, 0:W].rearrange("p (b k) -> p b k", k=16)
        u3 = u0[:, 0:W].rearrange("p (b k) -> p b k", k=16)
        rb = r_sb.rearrange("p (b o) -> p b o", o=1).broadcast_to((128, nb, 16))
        mb = m_sb.rearrange("p (b o) -> p b o", o=1).broadcast_to((128, nb, 16))
        e.tensor_tensor(u3, src3, rb, op=Alu.mult)
        e.drain()
        e.scalar_tensor_tensor(u0[:, 0:W], u0[:, 0:W], -1.0, u0[:, 0:W],
                               op0=Alu.mult, op1=Alu.max)
        e.drain()
        e.tensor_scalar(bA[:, 0:W], u0[:, 0:W], 0.25, None, Alu.is_gt)
        e.tensor_scalar(bB[:, 0:W], u0[:, 0:W], 0.75, None, Alu.is_gt)
        e.drain()
        for thr, wgt in ((1.25, None), (1.75, None), (2.5, 2.0), (3.5, 2.0),
                         (5.0, 4.0), (None, None)):
            e.tensor_tensor(bA[:, 0:W], bA[:, 0:W], bB[:, 0:W], op=Alu.add)
            e.drain()
            if thr is not None:
                if wgt is None:
                    e.tensor_scalar(bB[:, 0:W], u0[:, 0:W], thr, None, Alu.is_gt)
                else:
                    e.tensor_scalar(bB[:, 0:W], u0[:, 0:W], thr, wgt, Alu.is_gt,
                                    Alu.mult)
                e.drain()
        e.tensor_tensor(bA[:, 0:W], bA[:, 0:W], sgnb[:, 0:W], op=Alu.mult)
        e.drain()
        dst3 = dst[:, 0:W].rearrange("p (b k) -> p b k", k=16)
        bA3 = bA[:, 0:W].rearrange("p (b k) -> p b k", k=16)
        e.tensor_tensor(dst3, bA3, mb, op=Alu.mult)
        e.drain()

    def quant_fp4(e, dst, src, u0, bA, bB, bC, m_sb, r_sb, sgnb, W):
        quant_fp4_scales(e, src, m_sb, r_sb, W)
        quant_fp4_fast(e, dst, src, u0, bA, bB, bC, m_sb, r_sb, sgnb, W)

    def ln_stats_vector(e, xt, cAC, ac_rstd_val):
        e.tensor_reduce(stat[:, 0:1], xt, axis=AX.X, op=Alu.add)
        e.drain()
        e.tensor_scalar(stat[:, 2:3], stat[:, 0:1], 1.0 / D, None, Alu.mult)
        e.drain()
        e.scalar_tensor_tensor(stat[:, 4:5], stat[:, 2:3], -1.0, stat[:, 2:3],
                               op0=Alu.mult, op1=Alu.mult)
        e.drain()

    AR.reset()
    dl0 = fresh()
    dl0b = fresh()
    cGP0 = fresh()
    cPE0 = fresh()
    cVE0 = fresh()
    with nc.Block() as blk:
        @blk.gpsimd
        def _(e):
            e.memset(ident[:], 0.0)
            e.drain()
            e.affine_select(out=ident[:], in_=ident[:], compare_op=Alu.not_equal,
                            fill=1.0, base=0, pattern=[[-1, 128]], channel_multiplier=1)
            e.memset(identb[:], 0.0)
            e.drain()
            e.affine_select(out=identb[:], in_=identb[:], compare_op=Alu.not_equal,
                            fill=1.0, base=0, pattern=[[-1, 128]], channel_multiplier=1)
            e.memset(ones1[:], 1.0)
            e.memset(cm8[:], EXP_SHIFT)
            e.drain()
            cGP0.mark(e)

        @blk.sync
        def _(e):
            wait_stores(e)
            dl0.inc(e.dma_start(out=bvrow[0:1, :], in_=bv_h.rearrange("(o d) -> o d", o=1)), 16)
            dl0.inc(e.dma_start(
                out=bq_sb[:, :],
                in_=bqk_h.rearrange("(a h d t) -> (h d) a t", a=4, h=2, d=64, t=2)[:, :, 0],
                allow_slow_non_contiguous=True), 16)
            dl0.inc(e.dma_start(
                out=bk_sb[:, :],
                in_=bqk_h.rearrange("(a h d t) -> (h d) a t", a=4, h=2, d=64, t=2)[:, :, 1],
                allow_slow_non_contiguous=True), 16)
            dl0.inc(e.dma_start(out=ws_sb[:, :],
                                in_=Wsc_h[0:1, :].broadcast_to((128, 1))), 16)

    bcast_jobs = [(bout_h, bc_bout), (lng_h, bc_g), (lnb_h, bc_b), (b1_h, bc_b1)]
    with nc.Block() as blk:
        @blk.sync
        def _(e):
            wait_stores(e)
            for j, (src, dst) in enumerate(bcast_jobs):
                if j > 0:
                    e.wait_ge(cPE0.sem, 2 * j)
                dl0b.inc(e.dma_start(out=rows[0:1, :], in_=src.rearrange("(o d) -> o d", o=1)), 16)

        @blk.tensor
        def _(e):
            e.wait_ge(cGP0.sem, 1)
            for j in range(4):
                e.wait_ge(dl0b.sem, 16 * (j + 1))
                if j > 0:
                    e.wait_ge(cVE0.sem, 2 * j)
                for h in range(2):
                    cPE0.inc(e.matmul(pbank[h][:, :], ones1[:, :],
                                      rows[0:1, h * 512:(h + 1) * 512],
                                      start=True, stop=True))

        @blk.vector
        def _(e):
            for j, (src, dst) in enumerate(bcast_jobs):
                e.wait_ge(cPE0.sem, 2 * (j + 1))
                e.tensor_copy(dst[:, 0:512], pbank[0][:, :])
                cVE0.mark(e)
                e.tensor_copy(dst[:, 512:1024], pbank[1][:, :])
                e.drain()
                cVE0.mark(e)

    AR.reset()
    ARB.reset()
    wqin = AR.take(D)
    wqst = AR.take(D)
    wqk_in = AR.take(D)
    wqk_q = AR.take(VD)
    wqk_k = AR.take(VD)
    dec_f = AR.take(D)
    dec_sub = AR.take(D)
    dec_mask = AR.take(D)
    dec_val = AR.take(D)
    w8b = ARB.take(D)
    w8tb = ARB.take(D)
    ld1 = fresh()
    cPE1 = fresh()
    cVE1 = fresh()
    stv1 = fresh()
    all_store_ctrs.append(stv1)
    stv1b = fresh()
    all_store_ctrs.append(stv1b)

    with nc.Block() as blk:
        @blk.sync
        def _(e):
            wait_stores(e)
            for c in range(DCH):
                if c > 0:
                    e.wait_ge(cVE1.sem, 3 * c - 1)
                ld1.inc(e.dma_start(out=wqin[:, :],
                                    in_=Wq1_h[c * 128:(c + 1) * 128, :]), 16)
                ld1.inc(e.dma_start(out=u8a[:, :],
                                    in_=W18_h[c * 128:(c + 1) * 128, :]), 16)
                ld1.inc(e.dma_start(out=wqk_in[:, :],
                                    in_=Wqk_h[c * 128:(c + 1) * 128, :]), 16)

        @blk.tensor
        def _(e):
            for c in range(DCH):
                e.wait_ge(ld1.sem, 48 * (c + 1))
                if c > 0:
                    e.wait_ge(cVE1.sem, 3 * c)
                for f in range(DCH):
                    cPE1.inc(e.transpose(pbank[f][:, 0:128],
                                         wqin[:, f * 128:(f + 1) * 128], ident[:]))
                e.wait_ge(cVE1.sem, 3 * c + 2)
                for f in range(DCH):
                    cPE1.inc(e.transpose(pbank[f][:, 0:64].bitcast(BF16),
                                         w8b[:, f * 128:(f + 1) * 128], identb[:]))

        @blk.vector
        def _(e):
            for c in range(DCH):
                base_pe = 16 * c
                e.wait_ge(ld1.sem, 48 * (c + 1))
                if c > 0:
                    e.wait_ge(stv1.sem, 16 * (3 * c))
                wv = wqk_in[:, :].rearrange("p (h d t) -> p (h d) t", h=8,
                                            d=64, t=2)
                e.tensor_copy(wqk_q[:, :], wv[:, :, 0])
                e.tensor_copy(wqk_k[:, :], wv[:, :, 1])
                e.drain()
                for f in range(DCH):
                    e.wait_ge(cPE1.sem, base_pe + f + 1)
                    e.tensor_copy(wqst[:, f * 128:(f + 1) * 128], pbank[f][:, 0:128])
                e.drain()
                cVE1.mark(e)
                e.tensor_copy(u32a[:, :], u8a[:, :])
                e.drain()
                e.tensor_scalar(u32b[:, :], u32a[:, :], 7, None,
                                Alu.logical_shift_right)
                e.drain()
                e.tensor_copy(dec_f[:, :], u32b[:, :])
                e.tensor_scalar(u32a[:, :], u32a[:, :], 127, None, Alu.bitwise_and)
                e.drain()
                e.tensor_copy(dec_sub[:, :], u32a[:, :])
                e.tensor_scalar(u32b[:, :], u32a[:, :], 8, None, Alu.is_lt)
                e.drain()
                e.tensor_scalar(u32a[:, :], u32a[:, :], 960.0, 1048576.0, Alu.add,
                                Alu.mult)
                e.drain()
                e.tensor_scalar(dec_sub[:, :], dec_sub[:, :], float(2.0 ** -9),
                                None, Alu.mult)
                e.tensor_scalar(dec_f[:, :], dec_f[:, :], -2.0, 1.0, Alu.mult, Alu.add)
                e.drain()
                e.select(dec_val[:, :], u32b[:, :], dec_sub[:, :],
                         u32a[:, :].bitcast(F32), add_drain=True)
                e.drain()
                e.tensor_tensor(w8b[:, :], dec_val[:, :], dec_f[:, :], op=Alu.mult)
                e.drain()
                cVE1.mark(e)
                if c > 0:
                    e.wait_ge(stv1b.sem, 16 * c)
                for f in range(DCH):
                    e.wait_ge(cPE1.sem, base_pe + 8 + f + 1)
                    e.tensor_copy(w8tb[:, f * 128:(f + 1) * 128],
                                  pbank[f][:, 0:64].bitcast(BF16))
                e.drain()
                cVE1.mark(e)

        @blk.gpsimd
        def _(e):
            for c in range(DCH):
                e.wait_ge(cVE1.sem, 3 * c + 1)
                stv1.inc(e.dma_start(
                    out=scrWQT[:, c * 128:(c + 1) * 128].rearrange(
                        "(f p) o -> p f o", p=128),
                    in_=wqst[:, :].rearrange("p (f o) -> p f o", o=128)), 16)
                stv1.inc(e.dma_start(out=scrWQ2[c * 128:(c + 1) * 128, :],
                                     in_=wqk_q[:, :]), 16)
                stv1.inc(e.dma_start(out=scrWK2[c * 128:(c + 1) * 128, :],
                                     in_=wqk_k[:, :]), 16)
                e.wait_ge(cVE1.sem, 3 * c + 3)
                stv1b.inc(e.dma_start(
                    out=scrW1T[:, c * 128:(c + 1) * 128].rearrange(
                        "(f p) o -> p f o", p=128),
                    in_=w8tb[:, :].rearrange("p (f o) -> p f o", o=128)), 16)

    ready["WQT"] = (stv1, stv1.n)
    ready["W1T"] = (stv1b, stv1b.n)


    AR.reset()
    xin = [AR.take(D), AR.take(D)]
    xnt = AR.take(D)
    sq1 = AR.take(D)
    tst1 = AR.take(D)
    ld3 = [fresh(), fresh()]
    cPE3 = fresh()
    cVE3 = fresh()
    cAC3 = fresh()
    stv3 = fresh()
    all_store_ctrs.append(stv3)
    sta3 = fresh()
    all_store_ctrs.append(sta3)

    with nc.Block() as blk:
        @blk.sync
        def _(e):
            wait_stores(e)
            for c in range(TCH):
                sl = c % 2
                if c >= 2:
                    e.wait_ge(cAC3.sem, 2 * (c - 2) + 2)
                ld3[sl].inc(e.dma_start(out=xin[sl][:, :],
                                        in_=x_in[c * 128:(c + 1) * 128, :]), 16)

        @blk.vector
        def _(e):
            for c in range(TCH):
                sl = c % 2
                e.wait_ge(ld3[sl].sem, 16 * (c // 2 + 1))
                xt = xin[sl]
                e.tensor_reduce(stat[:, 0:1], xt[:, :], axis=AX.X, op=Alu.add)
                e.tensor_tensor(sq1[:, :], xt[:, :], xt[:, :], op=Alu.mult)
                e.drain()
                e.tensor_reduce(stat[:, 1:2], sq1[:, :], axis=AX.X, op=Alu.add)
                e.drain()
                e.tensor_scalar(stat[:, 2:3], stat[:, 0:1], 1.0 / D, None, Alu.mult)
                e.tensor_scalar(stat[:, 3:4], stat[:, 1:2], 1.0 / D, None, Alu.mult)
                e.drain()
                e.scalar_tensor_tensor(stat[:, 4:5], stat[:, 2:3], -1.0, stat[:, 2:3],
                                       op0=Alu.mult, op1=Alu.mult)
                e.drain()
                e.tensor_tensor(stat[:, 5:6], stat[:, 3:4], stat[:, 4:5], op=Alu.add)
                e.drain()
                e.tensor_scalar(stat[:, 13:14], stat[:, 5:6], LN_EPS, None, Alu.add)
                e.drain()
                e.reciprocal(stat[:, 14:15], stat[:, 13:14])
                e.drain()
                cVE3.mark(e)
                e.wait_ge(cAC3.sem, 2 * c + 1)
                e.scalar_tensor_tensor(stat[:, 7:8], stat[:, 2:3], -1.0, stat[:, 6:7],
                                       op0=Alu.mult, op1=Alu.mult)
                e.drain()
                cVE3.mark(e)
                if c > 0:
                    e.wait_ge(stv3.sem, 16 * c)
                for f in range(DCH):
                    e.wait_ge(cPE3.sem, 8 * c + f + 1)
                    e.tensor_copy(tst1[:, f * 128:(f + 1) * 128], pbank[f][:, 0:128])
                e.drain()
                cVE3.mark(e)

        @blk.scalar
        def _(e):
            for c in range(TCH):
                sl = c % 2
                e.wait_ge(cVE3.sem, 3 * c + 1)
                e.activation(stat[:, 6:7], stat[:, 14:15], AF.Sqrt)
                e.drain()
                cAC3.mark(e)
                e.wait_ge(cVE3.sem, 3 * c + 2)
                if c > 0:
                    e.wait_ge(sta3.sem, 16 * c)
                e.activation(xnt[:, :], xin[sl][:, :], AF.Identity,
                             bias=stat[:, 7:8], scale=stat[:, 6:7])
                e.drain()
                cAC3.mark(e)
                sta3.inc(e.dma_start(out=scrXN0tm[c * 128:(c + 1) * 128, :],
                                     in_=xnt[:, :]), 16)

        @blk.tensor
        def _(e):
            for c in range(TCH):
                e.wait_ge(cAC3.sem, 2 * c + 2)
                if c > 0:
                    e.wait_ge(cVE3.sem, 3 * c)
                for f in range(DCH):
                    cPE3.inc(e.transpose(pbank[f][:, 0:128],
                                         xnt[:, f * 128:(f + 1) * 128], ident[:]))

        @blk.gpsimd
        def _(e):
            for c in range(TCH):
                e.wait_ge(cVE3.sem, 3 * c + 3)
                stv3.inc(e.dma_start(
                    out=scrXN0fm[:, c * 128:(c + 1) * 128].rearrange(
                        "(f p) t -> p f t", p=128),
                    in_=tst1[:, :].rearrange("p (f t) -> p f t", t=128)), 16)

    ready["XN0tm"] = (sta3, sta3.n)
    ready["XN0fm"] = (stv3, stv3.n)

    AR.reset()
    wq_sb = [AR.take(VD) for _ in range(DCH)]
    wk_sb = [AR.take(VD) for _ in range(DCH)]
    wv_sb = [AR.take(VD) for _ in range(DCH)]
    xnf = [AR.take(DCH * 512), AR.take(DCH * 512)]
    qkst = [AR.take(512), AR.take(512)]
    vst = AR.take(NHEAD * 65)
    ldw6 = fresh()
    ldx6 = [fresh(), fresh()]
    cPE6 = fresh()
    cVE6 = fresh()
    cAC6 = fresh()
    stv6 = fresh()
    all_store_ctrs.append(stv6)
    sta6 = fresh()
    all_store_ctrs.append(sta6)

    with nc.Block() as blk:
        @blk.sync
        def _(e):
            wait_stores(e)
            wait_ready(e, "XN0fm", "WQT")
            for f in range(DCH):
                ldw6.inc(e.dma_start(out=wq_sb[f][:, :],
                                     in_=scrWQ2[f * 128:(f + 1) * 128, :]), 16)
                ldw6.inc(e.dma_start(out=wk_sb[f][:, :],
                                     in_=scrWK2[f * 128:(f + 1) * 128, :]), 16)
                ldw6.inc(e.dma_start(out=wv_sb[f][:, :],
                                     in_=Wv_h[f * 128:(f + 1) * 128, :]), 16)
            for ts in range(8):
                sl = ts % 2
                if ts >= 2:
                    e.wait_ge(cPE6.sem, 12 * (ts - 1))
                ldx6[sl].inc(e.dma_start(
                    out=xnf[sl][:, :],
                    in_=scrXN0fm[:, ts * 512:(ts + 1) * 512].rearrange(
                        "(f p) t -> p f t", p=128)), 16)

        @blk.tensor
        def _(e):
            e.wait_ge(dl0.sem, dl0.n)
            e.wait_ge(ldw6.sem, 16 * 3 * DCH)
            for ts in range(8):
                sl = ts % 2
                e.wait_ge(ldx6[sl].sem, 16 * (ts // 2 + 1))
                for qc in range(4):
                    for kind in range(2):
                        grp = 8 * ts + 2 * qc + kind
                        b = pbank[grp % 6]
                        if grp >= 6:
                            e.wait_ge(cAC6.sem, grp - 5)
                        wsb = wq_sb if kind == 0 else wk_sb
                        for f in range(DCH):
                            mm = e.matmul(b[:, :],
                                          wsb[f][:, qc * 128:(qc + 1) * 128],
                                          xnf[sl][:, f * 512:(f + 1) * 512],
                                          start=(f == 0), stop=(f == DCH - 1))
                        cPE6.inc(mm)
                for tc in range(4):
                    b = pbank[6 + tc % 2]
                    grp_v = 4 * ts + tc
                    if grp_v >= 2:
                        e.wait_ge(cVE6.sem, grp_v - 1)
                    for f in range(DCH):
                        e.matmul(b[:, :],
                                 xnf[sl][:, f * 512 + tc * 128:
                                         f * 512 + (tc + 1) * 128],
                                 wv_sb[f][:, :],
                                 start=(f == 0), stop=False)
                    mm = e.matmul(b[:, :], ones1[:, :], bvrow[0:1, :],
                                  start=False, stop=True)
                    cPE6.inc(mm)

        @blk.scalar
        def _(e):
            e.wait_ge(dl0.sem, dl0.n)
            for ts in range(8):
                for qc in range(4):
                    for kind in range(2):
                        grp = 8 * ts + 2 * qc + kind
                        b = pbank[grp % 6]
                        sl6 = grp % 2
                        if grp >= 2:
                            e.wait_ge(sta6.sem, 16 * grp)
                        e.wait_ge(cPE6.sem, 12 * ts + 2 * qc + kind + 1)
                        bias = (bq_sb if kind == 0 else bk_sb)[:, qc:qc + 1]
                        e.activation(qkst[sl6][:, :], b[:, :], AF.Identity,
                                     bias=bias, scale=1.0)
                        e.drain()
                        dst = (scrQ if kind == 0 else scrK)
                        sta6.inc(e.dma_start(
                            out=dst[qc * 128:(qc + 1) * 128,
                                    ts * 512:(ts + 1) * 512],
                            in_=qkst[sl6][:, :]), 16)
                        cAC6.mark(e)

        @blk.vector
        def _(e):
            on = vst.rearrange("p (h j) -> p h j", j=65)
            e.wait_ge(ldw6.sem, 16 * 3 * DCH)
            e.memset(on[:, :, 64:65], 1.0)
            e.drain()
            for ts in range(8):
                for tc in range(4):
                    b = pbank[6 + tc % 2]
                    gv = 4 * ts + tc
                    if gv >= 1:
                        e.wait_ge(stv6.sem, 16 * gv)
                    e.wait_ge(cPE6.sem, 12 * ts + 8 + tc + 1)
                    e.tensor_copy(on[:, :, 0:64],
                                  b[:, :].rearrange("p (h j) -> p h j", j=64))
                    e.drain()
                    cVE6.mark(e)

        @blk.gpsimd
        def _(e):
            for gv in range(32):
                e.wait_ge(cVE6.sem, gv + 1)
                ts, tc = gv // 4, gv % 4
                stv6.inc(e.dma_start(
                    out=scrV[ts * 512 + tc * 128: ts * 512 + (tc + 1) * 128, :],
                    in_=vst[:, :]), 16)

    ready["Q"] = (sta6, sta6.n)
    ready["K"] = (sta6, sta6.n)
    ready["V"] = (stv6, stv6.n)

    AR.reset()
    ARB.reset()
    qcb = [AR.take(2048), AR.take(2048)]
    kcb = AR.take(2048)
    vtb = AR.take(16 * 130)
    pTa = AR.take(16 * 512)
    pTb1 = AR.take(8 * 512)
    pTb2 = AR.take(8 * 512)
    ost = AR.take(520)

    def pT_sl(hh, kt, lo, hi):
        if hh == 0:
            return pTa[:, kt * 512 + lo:kt * 512 + hi]
        base = pTb1 if kt < 8 else pTb2
        return base[:, (kt % 8) * 512 + lo:(kt % 8) * 512 + hi]

    bgwin = [AR.take(1408), AR.take(1408)]
    bgu0 = AR.take(1408)
    bgqv = [AR.take(1408) for _ in range(4)]
    bgtst = AR.take(1408)
    bgmm = AR.take(88)
    bgrr = AR.take(88)
    bgA = ARB.take(1408)
    bgB = ARB.take(1408)
    bgsg = [ARB.take(1408), ARB.take(1408)]

    ld7 = [fresh(), fresh()]
    cPE7 = fresh()
    cVE7 = fresh()
    cAC7 = fresh()
    cAC7b = fresh()
    stv7 = fresh()
    all_store_ctrs.append(stv7)
    ldb = [fresh(), fresh()]
    cBSg = fresh()
    cBQ = fresh()
    cBT = fresh()
    cBE = fresh()
    sbst = fresh()
    all_store_ctrs.append(sbst)

    ajobs = [(s, cpair) for s in range(2) for cpair in range(4)]
    PE_J = 160
    AC_J = 128
    VE_J = 32

    bgjobs = []
    for c in range(ICH):
        bgjobs.append(("g", 1024, Wg_h[c * 128:(c + 1) * 128, :],
                       scrQWG[:, c * 128:(c + 1) * 128]))
    for c in range(ICH):
        bgjobs.append(("u", 1024, Wu_h[c * 128:(c + 1) * 128, :],
                       scrQWU[:, c * 128:(c + 1) * 128]))
    for c in range(TCH):
        bgjobs.append(("qx", 1024, scrXN0tm[c * 128:(c + 1) * 128, :],
                       scrQXfm[:, c * 128:(c + 1) * 128]))
    for c in range(DCH):
        for hh2 in range(2):
            bgjobs.append(("d", 1408,
                           Wd_h[c * 128:(c + 1) * 128,
                                hh2 * 1408:(hh2 + 1) * 1408],
                           scrQWD[hh2 * 1408:(hh2 + 1) * 1408,
                                  c * 128:(c + 1) * 128]))
    NBG = len(bgjobs)
    tile_cum = [0]
    for (_, W, _, _) in bgjobs:
        tile_cum.append(tile_cum[-1] + W // 128)
    NUNITS = 32
    ucut = [NBG * (u + 1) // NUNITS for u in range(NUNITS)]
    ucut = [0] + ucut

    def unit_jobs(u):
        if u < 0 or u >= NUNITS:
            return range(0, 0)
        return range(ucut[u], ucut[u + 1])

    with nc.Block() as blk:
        @blk.sync
        def _(e):
            wait_stores(e)
            wait_ready(e, "Q", "K", "V", "XN0tm")
            for ji, (s, cp) in enumerate(ajobs):
                sl = ji % 2
                if ji >= 2:
                    e.wait_ge(cPE7.sem, PE_J * (ji - 1))
                ld7[sl].inc(e.dma_start(
                    out=qcb[sl][:, :],
                    in_=scrQ[cp * 128:(cp + 1) * 128,
                             s * 2048:(s + 1) * 2048]), 16)
                if ji >= 1:
                    e.wait_ge(cPE7.sem, PE_J * ji - 8)
                ld7[sl].inc(e.dma_start(
                    out=kcb[:, :],
                    in_=scrK[cp * 128:(cp + 1) * 128,
                             s * 2048:(s + 1) * 2048]), 16)
                if ji >= 1:
                    e.wait_ge(cPE7.sem, PE_J * ji)
                ld7[sl].inc(e.dma_start(
                    out=vtb[:, :],
                    in_=scrV[s * 2048:(s + 1) * 2048,
                             cp * 130:(cp + 1) * 130].rearrange(
                        "(kt p) j -> p kt j", p=128)), 16)
                for u in range(4 * ji, 4 * ji + 4):
                    for k in unit_jobs(u):
                        slb = k % 2
                        if k >= 2:
                            e.wait_ge(cBQ.sem, k - 1)
                        W = bgjobs[k][1]
                        ldb[slb].inc(e.dma_start(out=bgwin[slb][:, 0:W],
                                                 in_=bgjobs[k][2]), 16)

        @blk.tensor
        def _(e):
            for ji, (s, cp) in enumerate(ajobs):
                sl = ji % 2
                e.wait_ge(ld7[sl].sem, 48 * (ji // 2 + 1))
                pe_base = PE_J * ji
                ac_base = AC_J * ji
                for qs in range(4):
                    u = 4 * ji + qs
                    for kt in range(16):
                        for hh in range(2):
                            b = pbank[2 * (kt % 2) + hh]
                            grp = 32 * qs + 2 * kt + hh
                            if grp >= 4 or ji > 0:
                                e.wait_ge(cAC7.sem, ac_base + grp - 3)
                            lo = 64 * hh
                            mm = e.matmul(b[:, :],
                                          kcb[lo:lo + 64,
                                              kt * 128:(kt + 1) * 128],
                                          qcb[sl][lo:lo + 64,
                                                  qs * 512:(qs + 1) * 512],
                                          start=True, stop=True)
                            cPE7.inc(mm)
                    e.wait_ge(cAC7.sem, ac_base + 32 * (qs + 1))
                    for hh in range(2):
                        for qq in range(4):
                            g = 4 * hh + qq
                            b = pbank[4 + g % 2]
                            pvglob = (4 * ji + qs) * 8 + g
                            if pvglob >= 2:
                                e.wait_ge(cAC7b.sem, pvglob - 1)
                            for kt in range(16):
                                mm = e.matmul(b[:, 0:65],
                                              pT_sl(hh, kt, qq * 128,
                                                    (qq + 1) * 128),
                                              vtb[:, :].rearrange(
                                                  "p (kt j) -> p kt j", j=130)[
                                                  :, kt, hh * 65:(hh + 1) * 65],
                                              start=(kt == 0), stop=(kt == 15))
                            cPE7.inc(mm)
                    for k in unit_jobs(u - 1):
                        e.wait_ge(cBQ.sem, k + 1)
                        for f in range(bgjobs[k][1] // 128):
                            t = tile_cum[k] + f
                            if t >= 2:
                                e.wait_ge(cBE.sem, t - 1)
                            cBT.inc(e.transpose(
                                pbank[6 + t % 2][:, 0:128],
                                bgqv[k % 4][:, f * 128:(f + 1) * 128], ident[:]))
            for u in (NUNITS - 1,):
                for k in unit_jobs(u):
                    e.wait_ge(cBQ.sem, k + 1)
                    for f in range(bgjobs[k][1] // 128):
                        t = tile_cum[k] + f
                        if t >= 2:
                            e.wait_ge(cBE.sem, t - 1)
                        cBT.inc(e.transpose(
                            pbank[6 + t % 2][:, 0:128],
                            bgqv[k % 4][:, f * 128:(f + 1) * 128], ident[:]))

        @blk.scalar
        def _(e):
            for ji, (s, cp) in enumerate(ajobs):
                pe_base = PE_J * ji
                for qs in range(4):
                    u = 4 * ji + qs
                    for kt in range(16):
                        for hh in range(2):
                            b = pbank[2 * (kt % 2) + hh]
                            e.wait_ge(cPE7.sem,
                                      pe_base + 40 * qs + 2 * kt + hh + 1)
                            e.activation(pT_sl(hh, kt, 0, 512), b[:, :],
                                         AF.Exp, bias=cm8[:, 0:1], scale=0.125)
                            e.drain()
                            cAC7.mark(e)
                    nst = 4 * u
                    for hh in range(2):
                        for qq in range(4):
                            g = 4 * hh + qq
                            b = pbank[4 + g % 2]
                            if nst > 0 and g == 0:
                                e.wait_ge(stv7.sem, 16 * nst)
                            e.wait_ge(cPE7.sem, pe_base + 40 * qs + 32 + g + 1)
                            e.activation(ost[:, qq * 130 + hh * 65:
                                             qq * 130 + hh * 65 + 65],
                                         b[:, 0:65], AF.Identity,
                                         bias=0.0, scale=1.0)
                            e.drain()
                            cAC7b.mark(e)


        @blk.vector
        def _(e):
            for ji, (s, cp) in enumerate(ajobs):
                pe_base = PE_J * ji
                for qs in range(4):
                    u = 4 * ji + qs
                    for k in unit_jobs(u - 1):
                        if k > 0:
                            e.wait_ge(sbst.sem, 16 * k)
                        for f in range(bgjobs[k][1] // 128):
                            t = tile_cum[k] + f
                            e.wait_ge(cBT.sem, t + 1)
                            e.tensor_copy(bgtst[:, f * 128:(f + 1) * 128],
                                          pbank[6 + t % 2][:, 0:128])
                            e.drain()
                            cBE.mark(e)
                    for k in unit_jobs(u):
                        slb = k % 2
                        W = bgjobs[k][1]
                        e.wait_ge(ldb[slb].sem, 16 * (k // 2 + 1))
                        if k >= 4:
                            e.wait_ge(cBT.sem, tile_cum[k - 3])
                        e.tensor_scalar(bgsg[0][:, 0:W], bgwin[slb][:, 0:W],
                                        0.0, 2.0, Alu.is_gt, Alu.mult)
                        e.drain()
                        e.tensor_scalar(bgsg[0][:, 0:W], bgsg[0][:, 0:W], 1.0,
                                        None, Alu.subtract)
                        e.drain()
                        quant_fp4_scales(e, bgwin[slb][:, 0:W],
                                         bgmm[:, 0:W // 16], bgrr[:, 0:W // 16],
                                         W)
                        quant_fp4_fast(e, bgqv[k % 4], bgwin[slb], bgu0,
                                       bgA, bgB, None, bgmm[:, 0:W // 16],
                                       bgrr[:, 0:W // 16], bgsg[0], W)
                        cBQ.mark(e)
            for u in (NUNITS,):
                for k in unit_jobs(u - 1):
                    if k > 0:
                        e.wait_ge(sbst.sem, 16 * k)
                    for f in range(bgjobs[k][1] // 128):
                        t = tile_cum[k] + f
                        e.wait_ge(cBT.sem, t + 1)
                        e.tensor_copy(bgtst[:, f * 128:(f + 1) * 128],
                                      pbank[6 + t % 2][:, 0:128])
                        e.drain()
                        cBE.mark(e)

        @blk.gpsimd
        def _(e):
            for ji, (s, cp) in enumerate(ajobs):
                for qs in range(4):
                    u = 4 * ji + qs
                    for qq in range(4):
                        e.wait_ge(cAC7b.sem, 8 * u + 4 + qq + 1)
                        tok0 = s * 2048 + qs * 512 + qq * 128
                        stv7.inc(e.dma_start(
                            out=scrO[tok0:tok0 + 128,
                                     cp * 130:(cp + 1) * 130],
                            in_=ost[:, qq * 130:(qq + 1) * 130]), 16)
                    for k in unit_jobs(u - 1):
                        e.wait_ge(cBE.sem, tile_cum[k + 1])
                        W = bgjobs[k][1]
                        sbst.inc(e.dma_start(
                            out=bgjobs[k][3].rearrange("(f p) o -> p f o", p=128),
                            in_=bgtst[:, 0:W].rearrange("p (f o) -> p f o",
                                                        o=128)), 16)
            for u in (NUNITS,):
                for k in unit_jobs(u - 1):
                    e.wait_ge(cBE.sem, tile_cum[k + 1])
                    W = bgjobs[k][1]
                    sbst.inc(e.dma_start(
                        out=bgjobs[k][3].rearrange("(f p) o -> p f o", p=128),
                        in_=bgtst[:, 0:W].rearrange("p (f o) -> p f o",
                                                    o=128)), 16)

    ready["O"] = (stv7, stv7.n)
    ready["QWG"] = (sbst, sbst.n)
    ready["QWU"] = (sbst, sbst.n)
    ready["QWD"] = (sbst, sbst.n)
    ready["QXfm"] = (sbst, sbst.n)

    AR.reset()
    isl_sizes = [512] * 5 + [256]
    isl_off = [0, 512, 1024, 1536, 2048, 2560]
    wgu = [AR.take(DCH * 512), AR.take(DCH * 512)]
    wuu = [AR.take(DCH * 512), AR.take(DCH * 512)]
    qxb = [AR.take(D), AR.take(D)]
    hti = [AR.take(512), AR.take(512)]
    tmg = [AR.take(512), AR.take(512)]
    tA2 = AR.take(512)
    qh2 = [AR.take(512), AR.take(512)]
    tst2 = [AR.take(512), AR.take(512)]
    ARB.reset()
    bA2 = ARB.take(512)
    bB2 = ARB.take(512)
    sg2b = [ARB.take(512), ARB.take(512)]
    mm2 = AR.take(32)
    rr2 = AR.take(32)
    ldw = [fresh(), fresh()]
    ldq = [fresh(), fresh()]
    cPE4 = fresh()
    cPT4 = fresh()
    cVE4 = fresh()

    def _h4(j):
        return 6 * (j // 2) + 1 + (j % 2)

    def _q4(j):
        return 6 * (j // 2) + 3 + (j % 2)

    def _e4(j):
        return 6 * (j // 2) + 5 + (j % 2)

    cAC4 = fresh()
    stv4 = fresh()
    all_store_ctrs.append(stv4)
    NJ2 = len(isl_sizes) * TCH

    def _isz(j):
        return isl_sizes[j // TCH]

    with nc.Block() as blk:
        @blk.sync
        def _(e):
            wait_stores(e)
            wait_ready(e, "QWG", "QWU", "QXfm")
            j = 0
            for i, isz in enumerate(isl_sizes):
                slw = i % 2
                if i >= 2:
                    e.wait_ge(cPE4.sem, 32 * (i - 1))
                ldw[slw].inc(e.dma_start(
                    out=wgu[slw][:, 0:DCH * isz],
                    in_=scrQWG[:, isl_off[i]:isl_off[i] + isz].rearrange(
                        "(f p) i -> p f i", p=128)), 16)
                ldw[slw].inc(e.dma_start(
                    out=wuu[slw][:, 0:DCH * isz],
                    in_=scrQWU[:, isl_off[i]:isl_off[i] + isz].rearrange(
                        "(f p) i -> p f i", p=128)), 16)
                for c in range(TCH):
                    slq = j % 2
                    if j >= 2:
                        e.wait_ge(cPE4.sem, j - 1)
                    ldq[slq].inc(e.dma_start(
                        out=qxb[slq][:, :],
                        in_=scrQXfm[:, c * 128:(c + 1) * 128].rearrange(
                            "(f p) t -> p f t", p=128)), 16)
                    j += 1

        @blk.tensor
        def _(e):
            for j in range(NJ2 + 2):
                if j < NJ2:
                    i, c = j // TCH, j % TCH
                    isz = isl_sizes[i]
                    slw, slq = i % 2, j % 2
                    e.wait_ge(ldw[slw].sem, 32 * (i // 2 + 1))
                    e.wait_ge(ldq[slq].sem, 16 * (j // 2 + 1))
                    if j >= 2:
                        e.wait_ge(cVE4.sem, _h4(j - 2))
                    bg = pbank[2 * (j % 2)]
                    bu = pbank[2 * (j % 2) + 1]
                    for f in range(DCH):
                        e.matmul(bg[:, 0:isz], qxb[slq][:, f * 128:(f + 1) * 128],
                                 wgu[slw][:, f * isz:(f + 1) * isz],
                                 start=(f == 0), stop=(f == DCH - 1))
                    for f in range(DCH):
                        mm = e.matmul(bu[:, 0:isz],
                                      qxb[slq][:, f * 128:(f + 1) * 128],
                                      wuu[slw][:, f * isz:(f + 1) * isz],
                                      start=(f == 0), stop=(f == DCH - 1))
                    cPE4.inc(mm)
                if j >= 2:
                    k = j - 2
                    ksz = _isz(k)
                    e.wait_ge(cVE4.sem, _q4(k))
                    if k >= 1:
                        e.wait_ge(cVE4.sem, _e4(k - 1))
                    for f in range(ksz // 128):
                        mm = e.transpose(pbank[4 + f % 4][:, 0:128],
                                         qh2[k % 2][:, f * 128:(f + 1) * 128],
                                         ident[:])
                    cPT4.inc(mm)

        @blk.scalar
        def _(e):
            for j in range(NJ2):
                isz = _isz(j)
                bg = pbank[2 * (j % 2)]
                if j % 2 == 0:
                    e.wait_ge(cPE4.sem, min(j + 2, NJ2))
                    if j >= 2:
                        e.wait_ge(cVE4.sem, _h4(j - 1))
                e.activation(tmg[j % 2][:, 0:isz], bg[:, 0:isz], AF.Silu)
                e.drain()
                cAC4.mark(e)

        @blk.vector
        def _(e):
            for m in range(NJ2 // 2):
                j0 = 2 * m
                e.wait_ge(cAC4.sem, j0 + 2)
                for j in (j0, j0 + 1):
                    isz = _isz(j)
                    bu = pbank[2 * (j % 2) + 1]
                    e.tensor_tensor(hti[j % 2][:, 0:isz], tmg[j % 2][:, 0:isz],
                                    bu[:, 0:isz], op=Alu.mult)
                    e.drain()
                    cVE4.mark(e)
                for j in (j0, j0 + 1):
                    isz = _isz(j)
                    e.tensor_scalar(sg2b[j % 2][:, 0:isz], hti[j % 2][:, 0:isz],
                                    0.0, 2.0, Alu.is_gt, Alu.mult)
                    e.drain()
                    e.tensor_scalar(sg2b[j % 2][:, 0:isz], sg2b[j % 2][:, 0:isz],
                                    1.0, None, Alu.subtract)
                    e.drain()
                for j in (j0, j0 + 1):
                    isz = _isz(j)
                    if j >= 2:
                        e.wait_ge(cPT4.sem, j - 1)
                    quant_fp4(e, qh2[j % 2], hti[j % 2], tA2, bA2, bB2, None,
                              mm2[:, 0:isz // 16], rr2[:, 0:isz // 16],
                              sg2b[j % 2], isz)
                    cVE4.mark(e)
                for j in (j0, j0 + 1):
                    isz = _isz(j)
                    e.wait_ge(cPT4.sem, j + 1)
                    if j > 0:
                        e.wait_ge(stv4.sem, 16 * j)
                    for f in range(isz // 128):
                        e.tensor_copy(tst2[j % 2][:, f * 128:(f + 1) * 128],
                                      pbank[4 + f % 4][:, 0:128])
                    e.drain()
                    cVE4.mark(e)

        @blk.gpsimd
        def _(e):
            for j in range(NJ2):
                i = j // TCH
                c = j % TCH
                isz = isl_sizes[i]
                e.wait_ge(cVE4.sem, _e4(j))
                stv4.inc(e.dma_start(
                    out=scrQH[isl_off[i]:isl_off[i] + isz,
                              c * 128:(c + 1) * 128].rearrange(
                        "(f p) t -> p f t", p=128),
                    in_=tst2[j % 2][:, 0:isz].rearrange("p (f t) -> p f t",
                                                        t=128)), 16)

    ready["QH"] = (stv4, stv4.n)

    AR.reset()
    wd_tiles = [AR.take(D) for _ in range(ICH)]
    qh_sl_a = AR.take(11 * 512)
    qh_sl_b = AR.take(11 * 512)
    g0st = [AR.take(D), AR.take(D)]

    def qh_chunk(ic, lo, hi):
        base = qh_sl_a if ic < 11 else qh_sl_b
        return base[:, (ic % 11) * 512 + lo:(ic % 11) * 512 + hi]
    ldwd = fresh()
    ldqh = fresh()
    cPE5 = fresh()
    cAC5 = fresh()
    sta5 = fresh()
    all_store_ctrs.append(sta5)

    with nc.Block() as blk:
        @blk.sync
        def _(e):
            wait_stores(e)
            wait_ready(e, "QWD", "QH")
            for ic in range(ICH):
                ldwd.inc(e.dma_start(out=wd_tiles[ic][:, :],
                                     in_=scrQWD[ic * 128:(ic + 1) * 128, :]), 16)
            for tt in range(8):
                if tt > 0:
                    e.wait_ge(cPE5.sem, 8 * tt)
                ldqh.inc(e.dma_start(
                    out=qh_sl_a[:, :],
                    in_=scrQH[0:11 * 128, tt * 512:(tt + 1) * 512].rearrange(
                        "(f p) t -> p f t", p=128)), 16)
                ldqh.inc(e.dma_start(
                    out=qh_sl_b[:, :],
                    in_=scrQH[11 * 128:22 * 128,
                              tt * 512:(tt + 1) * 512].rearrange(
                        "(f p) t -> p f t", p=128)), 16)

        @blk.tensor
        def _(e):
            e.wait_ge(ldwd.sem, 16 * ICH)
            for tt in range(8):
                e.wait_ge(ldqh.sem, 32 * (tt + 1))
                if tt > 0:
                    e.wait_ge(cAC5.sem, 12 * tt)
                for tc in range(4):
                    for hh in range(2):
                        b = pbank[2 * tc + hh]
                        for ic in range(ICH):
                            mm = e.matmul(b[:, :],
                                          qh_chunk(ic, tc * 128, (tc + 1) * 128),
                                          wd_tiles[ic][:, hh * 512:(hh + 1) * 512],
                                          start=(ic == 0), stop=(ic == ICH - 1))
                        cPE5.inc(mm)

        @blk.scalar
        def _(e):
            for tt in range(8):
                for tc in range(4):
                    sl = tc % 2
                    g04 = 4 * tt + tc
                    if g04 >= 2:
                        e.wait_ge(sta5.sem, 16 * g04)
                    for hh in range(2):
                        e.wait_ge(cPE5.sem, 8 * tt + 2 * tc + hh + 1)
                        e.activation(g0st[sl][:, hh * 512:(hh + 1) * 512],
                                     pbank[2 * tc + hh][:, :], AF.Silu)
                        e.drain()
                        cAC5.mark(e)
                    sta5.inc(e.dma_start(
                        out=scrG0[tt * 512 + tc * 128: tt * 512 + (tc + 1) * 128, :],
                        in_=g0st[sl][:, :]), 16)
                    cAC5.mark(e)

    ready["G0"] = (sta5, sta5.n)

    AR.reset()
    wout_sb = [AR.take(D) for _ in range(4)]
    oin = [AR.take(520), AR.take(520)]
    onrm = AR.take(VD)
    xn0in = [AR.take(D), AR.take(D)]
    g0in = [AR.take(D), AR.take(D)]
    x0in = [AR.take(D), AR.take(D)]
    ofm = AR.take(VD)
    t0 = AR.take(D)
    sqs = AR.take(D)
    z0 = AR.take(D)
    x1o = [AR.take(D), AR.take(D)]
    ld8 = [fresh(), fresh()]
    ldw8 = fresh()
    cPE8 = fresh()
    cVE8 = fresh()
    cAC8 = fresh()
    sta8 = fresh()
    all_store_ctrs.append(sta8)

    with nc.Block() as blk:
        @blk.sync
        def _(e):
            wait_stores(e)
            wait_ready(e, "O", "XN0tm", "G0")
            for f in range(4):
                ldw8.inc(e.dma_start(out=wout_sb[f][:, :],
                                     in_=Wout_h[f * 128:(f + 1) * 128, :]), 16)
            for c in range(TCH):
                sl = c % 2
                if c >= 2:
                    e.wait_ge(cVE8.sem, 6 * (c - 1))
                ld8[sl].inc(e.dma_start(out=oin[sl][:, :],
                                        in_=scrO[c * 128:(c + 1) * 128, :]), 16)
                ld8[sl].inc(e.dma_start(out=xn0in[sl][:, :],
                                        in_=scrXN0tm[c * 128:(c + 1) * 128, :]), 16)
                ld8[sl].inc(e.dma_start(out=g0in[sl][:, :],
                                        in_=scrG0[c * 128:(c + 1) * 128, :]), 16)
                ld8[sl].inc(e.dma_start(out=x0in[sl][:, :],
                                        in_=x_in[c * 128:(c + 1) * 128, :]), 16)

        @blk.tensor
        def _(e):
            e.wait_ge(ldw8.sem, 16 * 4)
            for c in range(TCH):
                sl = c % 2
                e.wait_ge(ld8[sl].sem, 64 * (c // 2 + 1))
                e.wait_ge(cVE8.sem, 6 * c + 1)
                if c > 0:
                    e.wait_ge(cVE8.sem, 6 * (c - 1) + 2)
                for f in range(4):
                    cPE8.inc(e.transpose(pbank[f][:, 0:128],
                                         onrm[:, f * 128:(f + 1) * 128],
                                         ident[:]))
                e.wait_ge(cVE8.sem, 6 * c + 2)
                if c > 0:
                    e.wait_ge(cVE8.sem, 6 * (c - 1) + 3)
                for hh in range(2):
                    b = pbank[4 + hh]
                    for f in range(4):
                        mm = e.matmul(b[:, :], ofm[:, f * 128:(f + 1) * 128],
                                      wout_sb[f][:, hh * 512:(hh + 1) * 512],
                                      start=(f == 0), stop=(f == 3))
                    cPE8.inc(mm)

        @blk.vector
        def _(e):
            for c in range(TCH):
                sl = c % 2
                e.wait_ge(ld8[sl].sem, 64 * (c // 2 + 1))
                o4 = oin[sl][:, :].rearrange("p (q h j) -> p q h j", q=4, h=2, j=65)
                e.reciprocal(stat[:, 16:24].rearrange("p (a b) -> p a b", a=4),
                             o4[:, :, :, 64])
                e.drain()
                e.tensor_tensor(
                    onrm[:, :].rearrange("p (q h j) -> p q h j", q=4, h=2, j=64),
                    o4[:, :, :, 0:64],
                    stat[:, 16:24].rearrange("p (a b) -> p a b", a=4)
                    .rearrange("p a b -> p a b 1" if False else "p (a b) -> p a b",
                               a=4)[:, :, :, None] if False else
                    stat[:, 16:24].rearrange("p (q h) -> p q h", q=4)
                    .broadcast_to((128, 4, 2, 64)),
                    op=Alu.mult)
                e.drain()
                cVE8.mark(e)
                for f in range(4):
                    e.wait_ge(cPE8.sem, 6 * c + f + 1)
                    e.tensor_copy(ofm[:, f * 128:(f + 1) * 128], pbank[f][:, 0:128])
                e.drain()
                cVE8.mark(e)
                e.wait_ge(cPE8.sem, 6 * c + 6)
                for hh in range(2):
                    e.tensor_tensor(t0[:, hh * 512:(hh + 1) * 512],
                                    pbank[4 + hh][:, :],
                                    bc_bout[:, hh * 512:(hh + 1) * 512], op=Alu.add)
                e.drain()
                e.tensor_reduce(stat[:, 0:1], t0[:, :], axis=AX.X, op=Alu.add)
                e.drain()
                e.tensor_scalar(stat[:, 2:3], stat[:, 0:1], 1.0 / D, None, Alu.mult)
                e.drain()
                e.scalar_tensor_tensor(stat[:, 4:5], stat[:, 2:3], -1.0,
                                       stat[:, 2:3], op0=Alu.mult, op1=Alu.mult)
                e.drain()
                cVE8.mark(e)
                e.wait_ge(cAC8.sem, 3 * c + 1)
                e.tensor_tensor(stat[:, 5:6], stat[:, 3:4], stat[:, 4:5],
                                op=Alu.add)
                e.drain()
                e.tensor_scalar(stat[:, 13:14], stat[:, 5:6], LN_EPS, None, Alu.add)
                e.drain()
                e.reciprocal(stat[:, 14:15], stat[:, 13:14])
                e.drain()
                cVE8.mark(e)
                e.wait_ge(cAC8.sem, 3 * c + 2)
                e.scalar_tensor_tensor(stat[:, 7:8], stat[:, 2:3], -1.0,
                                       stat[:, 6:7], op0=Alu.mult, op1=Alu.mult)
                e.drain()
                cVE8.mark(e)
                e.wait_ge(cAC8.sem, 3 * c + 3)
                e.tensor_tensor(z0[:, :], z0[:, :], bc_g[:, :], op=Alu.mult)
                e.drain()
                e.tensor_tensor(z0[:, :], z0[:, :], bc_b[:, :], op=Alu.add)
                e.drain()
                e.tensor_tensor(z0[:, :], z0[:, :], xn0in[sl][:, :], op=Alu.add)
                e.drain()
                e.scalar_tensor_tensor(z0[:, :], z0[:, :], RSQRT2, g0in[sl][:, :],
                                       op0=Alu.mult, op1=Alu.mult)
                e.drain()
                if c >= 2:
                    e.wait_ge(sta8.sem, 16 * c)
                e.tensor_tensor(x1o[sl][:, :], z0[:, :], x0in[sl][:, :], op=Alu.add)
                e.drain()
                cVE8.mark(e)

        @blk.gpsimd
        def _(e):
            for c in range(TCH):
                sl = c % 2
                e.wait_ge(cVE8.sem, 6 * c + 6)
                sta8.inc(e.dma_start(out=scrX1[c * 128:(c + 1) * 128, :],
                                     in_=x1o[sl][:, :]), 16)

        @blk.scalar
        def _(e):
            for c in range(TCH):
                e.wait_ge(cVE8.sem, 6 * c + 3)
                e.activation(sqs[:, :], t0[:, :], AF.Square, accum_out=stat[:, 1:2])
                e.drain()
                e.activation(stat[:, 3:4], stat[:, 1:2], AF.Copy, bias=0.0,
                             scale=1.0 / D)
                e.drain()
                cAC8.mark(e)
                e.wait_ge(cVE8.sem, 6 * c + 4)
                e.activation(stat[:, 6:7], stat[:, 14:15], AF.Sqrt)
                e.drain()
                cAC8.mark(e)
                e.wait_ge(cVE8.sem, 6 * c + 5)
                e.activation(z0[:, :], t0[:, :], AF.Identity, bias=stat[:, 7:8],
                             scale=stat[:, 6:7])
                e.drain()
                cAC8.mark(e)

    ready["X1"] = (sta8, sta8.n)

    AR.reset()
    x1in = [AR.take(D), AR.take(D)]
    xn1t = AR.take(D)
    sq9 = AR.take(D)
    tA9 = AR.take(D)
    tB9 = AR.take(D)
    tC9 = AR.take(D)
    tst9 = AR.take(D)
    ARB.reset()
    qib = ARB.take(D)
    tst9q = ARB.take(D)
    ld9 = [fresh(), fresh()]
    cPE9 = fresh()
    cVE9 = fresh()
    cAC9 = fresh()
    stv9 = fresh()
    all_store_ctrs.append(stv9)
    stv9b = fresh()
    all_store_ctrs.append(stv9b)
    sta9 = fresh()
    all_store_ctrs.append(sta9)

    with nc.Block() as blk:
        @blk.sync
        def _(e):
            wait_stores(e)
            wait_ready(e, "X1")
            for c in range(TCH):
                sl = c % 2
                if c >= 2:
                    e.wait_ge(cAC9.sem, 3 * (c - 2) + 2)
                ld9[sl].inc(e.dma_start(out=x1in[sl][:, :],
                                        in_=scrX1[c * 128:(c + 1) * 128, :]), 16)

        @blk.vector
        def _(e):
            for c in range(TCH):
                sl = c % 2
                e.wait_ge(ld9[sl].sem, 16 * (c // 2 + 1))
                xt = x1in[sl]
                e.tensor_reduce(stat[:, 0:1], xt[:, :], axis=AX.X, op=Alu.add)
                e.tensor_tensor(sq9[:, :], xt[:, :], xt[:, :], op=Alu.mult)
                e.drain()
                e.tensor_reduce(stat[:, 1:2], sq9[:, :], axis=AX.X, op=Alu.add)
                e.drain()
                e.tensor_scalar(stat[:, 2:3], stat[:, 0:1], 1.0 / D, None, Alu.mult)
                e.tensor_scalar(stat[:, 3:4], stat[:, 1:2], 1.0 / D, None, Alu.mult)
                e.drain()
                e.scalar_tensor_tensor(stat[:, 4:5], stat[:, 2:3], -1.0,
                                       stat[:, 2:3], op0=Alu.mult, op1=Alu.mult)
                e.drain()
                e.tensor_tensor(stat[:, 5:6], stat[:, 3:4], stat[:, 4:5], op=Alu.add)
                e.drain()
                e.tensor_scalar(stat[:, 13:14], stat[:, 5:6], LN_EPS, None, Alu.add)
                e.drain()
                e.reciprocal(stat[:, 14:15], stat[:, 13:14])
                e.drain()
                cVE9.mark(e)
                e.wait_ge(cAC9.sem, 3 * c + 1)
                e.scalar_tensor_tensor(stat[:, 7:8], stat[:, 2:3], -1.0,
                                       stat[:, 6:7], op0=Alu.mult, op1=Alu.mult)
                e.drain()
                cVE9.mark(e)
                e.wait_ge(cAC9.sem, 3 * c + 3)
                e.tensor_reduce(stat[:, 9:10], xn1t[:, :], axis=AX.X, op=Alu.max,
                                apply_absolute_value=True)
                e.drain()
                e.tensor_scalar(stat[:, 10:11], stat[:, 9:10], 1e-12,
                                float(np.float32(1.0) / np.float32(448.0)),
                                Alu.max, Alu.mult)
                e.drain()
                e.tensor_copy(xs_all[:, c:c + 1], stat[:, 10:11])
                e.reciprocal(stat[:, 11:12], stat[:, 10:11])
                e.drain()
                e.tensor_scalar(tA9[:, :], xn1t[:, :], stat[:, 11:12], None,
                                Alu.mult)
                e.drain()
                e.scalar_tensor_tensor(tB9[:, :], tA9[:, :], -1.0, tA9[:, :],
                                       op0=Alu.mult, op1=Alu.max)
                e.drain()
                e.tensor_scalar(tC9[:, :], tB9[:, :], float(2.0 ** -6),
                                float(2.0 ** -6), Alu.is_lt, Alu.mult)
                e.drain()
                e.tensor_tensor(tB9[:, :], tB9[:, :], tC9[:, :], op=Alu.add)
                e.drain()
                e.tensor_copy(u32a[:, :], tB9[:, :].bitcast(U32))
                e.drain()
                e.tensor_scalar(u32a[:, :], u32a[:, :], 0xFFF00000, None,
                                Alu.bitwise_and)
                e.drain()
                e.tensor_scalar(u32b[:, :], u32a[:, :], 23, None,
                                Alu.logical_shift_right)
                e.drain()
                e.tensor_scalar(u32b[:, :], u32b[:, :], 3.0, 8388608.0,
                                Alu.subtract, Alu.mult)
                e.drain()
                e.scalar_tensor_tensor(tA9[:, :], u32b[:, :].bitcast(F32), 0.5,
                                       u32a[:, :].bitcast(F32),
                                       op0=Alu.mult, op1=Alu.add)
                e.drain()
                e.tensor_tensor(tA9[:, :], tB9[:, :], tA9[:, :], op=Alu.is_ge)
                e.drain()
                e.tensor_tensor(tA9[:, :], tA9[:, :], u32b[:, :].bitcast(F32),
                                op=Alu.mult)
                e.drain()
                e.tensor_tensor(tA9[:, :], tA9[:, :], u32a[:, :].bitcast(F32),
                                op=Alu.add)
                e.drain()
                e.tensor_tensor(tA9[:, :], tA9[:, :], tC9[:, :], op=Alu.subtract)
                e.drain()
                e.tensor_tensor(qib[:, :], tA9[:, :], sq9[:, :], op=Alu.mult)
                e.drain()
                cVE9.mark(e)
                if c > 0:
                    e.wait_ge(stv9.sem, 16 * c)
                for f in range(DCH):
                    e.wait_ge(cPE9.sem, 16 * c + f + 1)
                    e.tensor_copy(tst9[:, f * 128:(f + 1) * 128],
                                  pbank[f][:, 0:128])
                e.drain()
                cVE9.mark(e)
                if c > 0:
                    e.wait_ge(stv9b.sem, 16 * c)
                for f in range(DCH):
                    e.wait_ge(cPE9.sem, 16 * c + 8 + f + 1)
                    e.tensor_copy(tst9q[:, f * 128:(f + 1) * 128],
                                  pbank[f][:, 0:64].bitcast(BF16))
                e.drain()
                cVE9.mark(e)

        @blk.gpsimd
        def _(e):
            for c in range(TCH):
                e.wait_ge(cVE9.sem, 5 * c + 4)
                stv9.inc(e.dma_start(
                    out=scrXN1fm[:, c * 128:(c + 1) * 128].rearrange(
                        "(f p) t -> p f t", p=128),
                    in_=tst9[:, :].rearrange("p (f t) -> p f t", t=128)), 16)
                e.wait_ge(cVE9.sem, 5 * c + 5)
                stv9b.inc(e.dma_start(
                    out=scrQI[c, :, :].rearrange("(f p) t -> p f t", p=128),
                    in_=tst9q[:, :].rearrange("p (f t) -> p f t", t=128)), 16)

        @blk.scalar
        def _(e):
            for c in range(TCH):
                sl = c % 2
                e.wait_ge(cVE9.sem, 5 * c + 1)
                e.activation(stat[:, 6:7], stat[:, 14:15], AF.Sqrt)
                e.drain()
                cAC9.mark(e)
                e.wait_ge(cVE9.sem, 5 * c + 2)
                if c > 0:
                    e.wait_ge(sta9.sem, 16 * c)
                e.activation(xn1t[:, :], x1in[sl][:, :], AF.Identity,
                             bias=stat[:, 7:8], scale=stat[:, 6:7])
                e.drain()
                cAC9.mark(e)
                sta9.inc(e.dma_start(out=scrXN1tm[c * 128:(c + 1) * 128, :],
                                     in_=xn1t[:, :]), 16)
                e.activation(sq9[:, :], xn1t[:, :], AF.Sign)
                e.drain()
                cAC9.mark(e)

        @blk.tensor
        def _(e):
            for c in range(TCH):
                e.wait_ge(cAC9.sem, 3 * c + 2)
                if c > 0:
                    e.wait_ge(cVE9.sem, 5 * c)
                for f in range(DCH):
                    cPE9.inc(e.transpose(pbank[f][:, 0:128],
                                         xn1t[:, f * 128:(f + 1) * 128], ident[:]))
                e.wait_ge(cVE9.sem, 5 * c + 4)
                for f in range(DCH):
                    cPE9.inc(e.transpose(pbank[f][:, 0:64].bitcast(BF16),
                                         qib[:, f * 128:(f + 1) * 128], identb[:]))

    ready["XN1tm"] = (sta9, sta9.n)
    ready["XN1fm"] = (stv9, stv9.n)
    ready["QI"] = (stv9b, stv9b.n)

    AR.reset()
    xn1s = [AR.take(D) for _ in range(16)]
    gst = AR.take(D)
    ldA = fresh()
    cPEa = fresh()
    cVEa = fresh()
    stva = fresh()
    all_store_ctrs.append(stva)

    with nc.Block() as blk:
        @blk.sync
        def _(e):
            wait_stores(e)
            wait_ready(e, "XN1tm")
            for s in range(2):
                if s == 1:
                    e.wait_ge(cPEa.sem, 16)
                for tloc in range(16):
                    ldA.inc(e.dma_start(
                        out=xn1s[tloc][:, :],
                        in_=scrXN1tm[(s * 16 + tloc) * 128:
                                     (s * 16 + tloc + 1) * 128, :]), 16)

        @blk.tensor
        def _(e):
            for s in range(2):
                e.wait_ge(ldA.sem, 16 * 16 * (s + 1))
                for ic in range(DCH):
                    for jh in range(2):
                        grp = 16 * s + 2 * ic + jh
                        b = pbank[grp % 6]
                        if grp >= 6:
                            e.wait_ge(cVEa.sem, grp - 5)
                        for tloc in range(16):
                            mm = e.matmul(b[:, :],
                                          xn1s[tloc][:, ic * 128:(ic + 1) * 128],
                                          xn1s[tloc][:, jh * 512:(jh + 1) * 512],
                                          start=(tloc == 0), stop=(tloc == 15))
                        cPEa.inc(mm)

        @blk.vector
        def _(e):
            for s in range(2):
                for ic in range(DCH):
                    g8 = 8 * s + ic
                    if g8 >= 1:
                        e.wait_ge(stva.sem, 16 * g8)
                    for jh in range(2):
                        b = pbank[(16 * s + 2 * ic + jh) % 6]
                        e.wait_ge(cPEa.sem, 16 * s + 2 * ic + jh + 1)
                        e.tensor_copy(gst[:, jh * 512:(jh + 1) * 512], b[:, :])
                        e.drain()
                        cVEa.mark(e)

        @blk.gpsimd
        def _(e):
            for s in range(2):
                for ic in range(DCH):
                    e.wait_ge(cVEa.sem, 16 * s + 2 * ic + 2)
                    stva.inc(e.dma_start(
                        out=scrG1[(s * DCH + ic) * 128:(s * DCH + ic + 1) * 128, :],
                        in_=gst[:, :]), 16)

    ready["G1"] = (stva, stva.n)

    AR.reset()
    wqt_sb = [AR.take(D) for _ in range(DCH)]
    xnf9 = [AR.take(DCH * 512), AR.take(DCH * 512)]
    qst9 = AR.take(512)
    ldwb = fresh()
    ldxb = [fresh(), fresh()]
    cPEb = fresh()
    cVEb = fresh()
    stvb = fresh()
    all_store_ctrs.append(stvb)

    with nc.Block() as blk:
        @blk.sync
        def _(e):
            wait_stores(e)
            wait_ready(e, "WQT", "XN1fm")
            for f in range(DCH):
                ldwb.inc(e.dma_start(out=wqt_sb[f][:, :],
                                     in_=scrWQT[f * 128:(f + 1) * 128, :]), 16)
            for ts in range(8):
                sl = ts % 2
                if ts >= 2:
                    e.wait_ge(cPEb.sem, 8 * (ts - 1))
                ldxb[sl].inc(e.dma_start(
                    out=xnf9[sl][:, :],
                    in_=scrXN1fm[:, ts * 512:(ts + 1) * 512].rearrange(
                        "(f p) t -> p f t", p=128)), 16)

        @blk.tensor
        def _(e):
            e.wait_ge(ldwb.sem, 16 * DCH)
            for ts in range(8):
                sl = ts % 2
                e.wait_ge(ldxb[sl].sem, 16 * (ts // 2 + 1))
                for oc in range(DCH):
                    b = pbank[oc % 4]
                    grp = 8 * ts + oc
                    if grp >= 4:
                        e.wait_ge(cVEb.sem, grp - 3)
                    for f in range(DCH):
                        mm = e.matmul(b[:, :],
                                      wqt_sb[f][:, oc * 128:(oc + 1) * 128],
                                      xnf9[sl][:, f * 512:(f + 1) * 512],
                                      start=(f == 0), stop=(f == DCH - 1))
                    cPEb.inc(mm)

        @blk.vector
        def _(e):
            for ts in range(8):
                for oc in range(DCH):
                    b = pbank[oc % 4]
                    g9 = 8 * ts + oc
                    if g9 >= 1:
                        e.wait_ge(stvb.sem, 64 * g9)
                    e.wait_ge(cPEb.sem, 8 * ts + oc + 1)
                    e.tensor_copy(qst9[:, :], b[:, :])
                    e.drain()
                    cVEb.mark(e)

        @blk.gpsimd
        def _(e):
            for ts in range(8):
                for oc in range(DCH):
                    e.wait_ge(cVEb.sem, 8 * ts + oc + 1)
                    for tc in range(4):
                        stvb.inc(e.dma_start(
                            out=scrQ1b[4 * ts + tc, oc * 128:(oc + 1) * 128, :],
                            in_=qst9[:, tc * 128:(tc + 1) * 128]), 16)

    ready["Q1"] = (stvb, stvb.n)

    AR.reset()
    gq = [AR.take(D) for _ in range(DCH)]
    q1in = [AR.take(D), AR.take(D)]
    x1in0 = [AR.take(D), AR.take(D)]
    g1pre = AR.take(D)
    g1t = AR.take(D)
    oout = [AR.take(D), AR.take(D)]
    ARB.reset()
    qiin = [AR.take(512).bitcast(BF16), AR.take(512).bitcast(BF16)]
    w1tin = [ARB.take(D) for _ in range(DCH)]
    ldc = [fresh(), fresh()]
    ldg = fresh()
    ldwc = fresh()
    cPEc = fresh()
    cVEc = fresh()
    cACc = fresh()
    stac = fresh()
    all_store_ctrs.append(stac)

    with nc.Block() as blk:
        @blk.sync
        def _(e):
            wait_stores(e)
            wait_ready(e, "G1", "Q1", "QI", "W1T", "X1")
            for f in range(DCH):
                ldwc.inc(e.dma_start(out=w1tin[f][:, :],
                                     in_=scrW1T[f * 128:(f + 1) * 128, :]), 16)
            for s in range(2):
                if s == 1:
                    e.wait_ge(cPEc.sem, 4 * 16)
                for f in range(DCH):
                    ldg.inc(e.dma_start(
                        out=gq[f][:, :],
                        in_=scrG1[(s * DCH + f) * 128:(s * DCH + f + 1) * 128, :]),
                        16)
                for cl in range(16):
                    c = s * 16 + cl
                    sl = c % 2
                    if c >= 2:
                        e.wait_ge(cVEc.sem, 2 * (c - 1))
                    ldc[sl].inc(e.dma_start(
                        out=q1in[sl][:, :],
                        in_=scrQ1b[c, :, :].rearrange("(f p) t -> p f t", p=128)),
                        16)
                    ldc[sl].inc(e.dma_start(
                        out=qiin[sl][:, :],
                        in_=scrQI[c, :, :].rearrange("(f p) t -> p f t", p=128)),
                        16)
                    ldc[sl].inc(e.dma_start(
                        out=x1in0[sl][:, :],
                        in_=scrX1[c * 128:(c + 1) * 128, :]), 16)

        @blk.tensor
        def _(e):
            e.wait_ge(ldwc.sem, 16 * DCH)
            for s in range(2):
                e.wait_ge(ldg.sem, 16 * DCH * (s + 1))
                for cl in range(16):
                    c = s * 16 + cl
                    sl = c % 2
                    e.wait_ge(ldc[sl].sem, 48 * (c // 2 + 1))
                    if c > 0:
                        e.wait_ge(cVEc.sem, 2 * c)
                    for hh in range(2):
                        b = pbank[hh]
                        for f in range(DCH):
                            mm = e.matmul(b[:, :],
                                          q1in[sl][:, f * 128:(f + 1) * 128],
                                          gq[f][:, hh * 512:(hh + 1) * 512],
                                          start=(f == 0), stop=(f == DCH - 1))
                        cPEc.inc(mm)
                    for hh in range(2):
                        b = pbank[2 + hh]
                        for f in range(DCH):
                            mm = e.matmul(b[:, :],
                                          qiin[sl][:, f * 128:(f + 1) * 128],
                                          w1tin[f][:, hh * 512:(hh + 1) * 512],
                                          start=(f == 0), stop=(f == DCH - 1))
                        cPEc.inc(mm)

        @blk.vector
        def _(e):
            e.wait_ge(dl0.sem, dl0.n)
            for c in range(TCH):
                sl = c % 2
                e.wait_ge(cPEc.sem, 4 * c + 4)
                e.tensor_tensor(stat[:, 12:13], xs_all[:, c:c + 1], ws_sb[:, :],
                                op=Alu.mult)
                e.drain()
                for hh in range(2):
                    e.scalar_tensor_tensor(g1pre[:, hh * 512:(hh + 1) * 512],
                                           pbank[2 + hh][:, :], stat[:, 12:13],
                                           bc_b1[:, hh * 512:(hh + 1) * 512],
                                           op0=Alu.mult, op1=Alu.add)
                e.drain()
                cVEc.mark(e)
                e.wait_ge(cACc.sem, c + 1)
                if c >= 2:
                    e.wait_ge(stac.sem, 16 * c)
                for hh in range(2):
                    e.scalar_tensor_tensor(oout[sl][:, hh * 512:(hh + 1) * 512],
                                           pbank[hh][:, :], 1.0,
                                           g1t[:, hh * 512:(hh + 1) * 512],
                                           op0=Alu.mult, op1=Alu.mult)
                e.drain()
                e.tensor_tensor(oout[sl][:, :], oout[sl][:, :], x1in0[sl][:, :],
                                op=Alu.add)
                e.drain()
                cVEc.mark(e)

        @blk.gpsimd
        def _(e):
            for c in range(TCH):
                sl = c % 2
                e.wait_ge(cVEc.sem, 2 * c + 2)
                stac.inc(e.dma_start(out=out_h[c * 128:(c + 1) * 128, :],
                                     in_=oout[sl][:, :]), 16)

        @blk.scalar
        def _(e):
            for c in range(TCH):
                e.wait_ge(cVEc.sem, 2 * c + 1)
                e.activation(g1t[:, :], g1pre[:, :], AF.Silu)
                e.drain()
                cACc.mark(e)

    with nc.Block() as blk:
        @blk.sync
        def _(e):
            wait_stores(e)
            e.wait_ge(stac.sem, stac.n)

    es.close()
    return nc


_CACHE = {}


def kernel(**inputs):
    if "nc" not in _CACHE:
        _CACHE["nc"] = build_program()
    nc = _CACHE["nc"]
    x = np.ascontiguousarray(np.asarray(inputs["L0_attn_x"], dtype=np.float32))
    w18 = np.asarray(inputs["L1_ffn_weight"]).view(np.uint8)
    wsc = np.asarray(inputs["L1_ffn_weight_scale"], dtype=np.float32).reshape(1, 1)
    common = {
        "Wqk": np.ascontiguousarray(np.asarray(inputs["L0_attn_W_qk"], np.float32)),
        "bqk": np.ascontiguousarray(np.asarray(inputs["L0_attn_b_qk"], np.float32)),
        "Wv": np.ascontiguousarray(np.asarray(inputs["L0_attn_W_v"], np.float32)),
        "bv": np.ascontiguousarray(np.asarray(inputs["L0_attn_b_v"], np.float32)),
        "Wout": np.ascontiguousarray(np.asarray(inputs["L0_attn_W_out"], np.float32)),
        "bout": np.ascontiguousarray(np.asarray(inputs["L0_attn_b_out"], np.float32)),
        "lng": np.ascontiguousarray(np.asarray(inputs["L0_attn_ln_gamma"], np.float32)),
        "lnb": np.ascontiguousarray(np.asarray(inputs["L0_attn_ln_beta"], np.float32)),
        "Wg": np.ascontiguousarray(np.asarray(inputs["L0_ffn_w_gate"], np.float32)),
        "Wu": np.ascontiguousarray(np.asarray(inputs["L0_ffn_w_up"], np.float32)),
        "Wd": np.ascontiguousarray(np.asarray(inputs["L0_ffn_w_down"], np.float32)),
        "Wq1": np.ascontiguousarray(np.asarray(inputs["L1_attn_W_q"], np.float32)),
        "W18": np.ascontiguousarray(w18),
        "Wsc": wsc,
        "b1v": np.ascontiguousarray(np.asarray(inputs["L1_ffn_bias"], np.float32)),
    }
    nper = B_FULL // NCORES
    in_maps = []
    for ci in range(NCORES):
        m = dict(common)
        m["x_in"] = np.ascontiguousarray(
            x[ci * nper:(ci + 1) * nper].reshape(T, D))
        in_maps.append(m)
    res = run_bass_kernel_spmd(nc, in_maps, core_ids=list(range(NCORES)))
    outs = [np.asarray(r["out"]).reshape(nper, S, D) for r in res.results]
    return np.concatenate(outs, axis=0)


# version 27
# speedup vs baseline: 1.0652x; 1.0652x over previous
import numpy as np
import ml_dtypes
from contextlib import ExitStack

import concourse.bass as bass
import concourse.mybir as mybir
from concourse.bass_utils import run_bass_kernel_spmd

F32 = mybir.dt.float32
BF16 = mybir.dt.bfloat16
U8 = mybir.dt.uint8
U32 = mybir.dt.uint32
Alu = mybir.AluOpType
AF = mybir.ActivationFunctionType
AX = mybir.AxisListType


from concourse import dve_ops as _dvo
from concourse.dve_op import DveOp as _DveOp
from concourse.dve_spec import (Spec as _Spec, Src0 as _S0, Src1 as _S1,
                                C0 as _C0, C1 as _C1, C2 as _C2,
                                Zero as _Z, One as _One, lower as _dve_lower)


def _fp4a_ref(in0, in1, s0, s1, imm2):
    a = in0 * in1
    v = a + a - s0
    return ((v > 0).astype(in0.dtype) + (v > 1)) + (v > s1)


def _fp4b_ref(in0, in1, s0, s1, imm2):
    a = in0 * in1
    v = a + a - s0
    t = (v > s1).astype(in0.dtype)
    return ((v > 0).astype(in0.dtype) + (t + t)) * in1


def _fp4c_ref(in0, in1, s0, s1, imm2):
    a = in0 * in1
    w = a + a
    t3 = (w > s1).astype(in0.dtype)
    s = (w > s0).astype(in0.dtype) + (t3 + t3)
    return (s + s) * in1


def _register_fp4_ops():
    have = {o.name for o in _dvo.OPS}
    _a = _S0 * _S1
    _v = _a + _a - _C0
    bodyA = ((_v > _Z) + (_v > _One)) + (_v > _C1)
    _t = _v > _C1
    bodyB = ((_v > _Z) + (_t + _t)) * _S1
    _w = _a + _a
    _t3 = _w > _C1
    _s = (_w > _C0) + (_t3 + _t3)
    bodyC = (_s + _s) * _S1
    defs = [("FP4A_ANT", bodyA, _fp4a_ref), ("FP4B_ANT", bodyB, _fp4b_ref),
            ("FP4C_ANT", bodyC, _fp4c_ref)]
    out = []
    for name, body, ref in defs:
        if name in have:
            out.append(next(o for o in _dvo.OPS if o.name == name))
            continue
        sha = {"v3": _dve_lower(_Spec(body=body), ver="v3", perf_en=False,
                                name=name).sha("v3")}
        op = _DveOp(name, _Spec(body=body), False, sha, reference=ref)
        if len(_dvo.OPS) < 31:
            _dvo.OPS.append(op)
            row = _dvo._CUSTOM_DVE_ROW_BASE + len(_dvo.OPS) - 1
        else:
            victim = next(i for i, o in enumerate(_dvo.OPS)
                          if o.name.startswith("DVEHASH"))
            _dvo.CUSTOM_DVE_SPECS.pop(_dvo.OPS[victim].name, None)
            _dvo.CUSTOM_DVE_REFS.pop(_dvo.OPS[victim].name, None)
            _dvo._SUB_OPCODE_FOR_NAME.pop(_dvo.OPS[victim].name, None)
            _dvo.OPS[victim] = op
            row = _dvo._CUSTOM_DVE_ROW_BASE + victim
        _dvo.CUSTOM_DVE_SPECS[name] = op.spec
        _dvo.CUSTOM_DVE_REFS[name] = op.ref
        _dvo._SUB_OPCODE_FOR_NAME[name] = row
        out.append(op)
    return out


FP4A_OP, FP4B_OP, FP4C_OP = _register_fp4_ops()

NCORES = 8
B_FULL, S, D = 16, 2048, 1024
T = (B_FULL // NCORES) * S
TCH = T // 128
DCH = D // 128
I_FFN = 2816
ICH = I_FFN // 128
NHEAD, DHEAD = 8, 64
VD = NHEAD * DHEAD
RSQRT2 = float(2.0 ** -0.5)
LN_EPS = 1e-5
EXP_SHIFT = -8.0


class Ctr:
    def __init__(self, sem):
        self.sem = sem
        self.n = 0

    def inc(self, instr, k=1):
        instr.then_inc(self.sem, k)
        self.n += k
        return self.n

    def mark(self, e):
        e.sem_inc(self.sem, 1)
        self.n += 1
        return self.n


class Arenas:
    def __init__(self, tensors):
        self.tensors = tensors
        self.off = [0] * len(tensors)

    def reset(self):
        self.off = [0] * len(self.tensors)

    def take(self, ncols):
        for i, (t, cap) in enumerate(self.tensors):
            if self.off[i] + ncols <= cap:
                o = self.off[i]
                self.off[i] += ncols
                return t[:, o:o + ncols]
        raise RuntimeError(f"arena overflow: need {ncols}, offs={self.off}")


def build_program():
    nc = bass.Bass("TRN2", target_bir_lowering=False)

    x_in = nc.dram_tensor("x_in", [T, D], F32, kind="ExternalInput").ap()
    Wqk_h = nc.dram_tensor("Wqk", [D, D], F32, kind="ExternalInput").ap()
    bqk_h = nc.dram_tensor("bqk", [D], F32, kind="ExternalInput").ap()
    Wv_h = nc.dram_tensor("Wv", [D, VD], F32, kind="ExternalInput").ap()
    bv_h = nc.dram_tensor("bv", [VD], F32, kind="ExternalInput").ap()
    Wout_h = nc.dram_tensor("Wout", [VD, D], F32, kind="ExternalInput").ap()
    bout_h = nc.dram_tensor("bout", [D], F32, kind="ExternalInput").ap()
    lng_h = nc.dram_tensor("lng", [D], F32, kind="ExternalInput").ap()
    lnb_h = nc.dram_tensor("lnb", [D], F32, kind="ExternalInput").ap()
    Wg_h = nc.dram_tensor("Wg", [I_FFN, D], F32, kind="ExternalInput").ap()
    Wu_h = nc.dram_tensor("Wu", [I_FFN, D], F32, kind="ExternalInput").ap()
    Wd_h = nc.dram_tensor("Wd", [D, I_FFN], F32, kind="ExternalInput").ap()
    Wq1_h = nc.dram_tensor("Wq1", [D, D], F32, kind="ExternalInput").ap()
    W18_h = nc.dram_tensor("W18", [D, D], U8, kind="ExternalInput").ap()
    Wsc_h = nc.dram_tensor("Wsc", [1, 1], F32, kind="ExternalInput").ap()
    b1_h = nc.dram_tensor("b1v", [D], F32, kind="ExternalInput").ap()
    out_h = nc.dram_tensor("out", [T, D], F32, kind="ExternalOutput").ap()

    scrXN0tm = nc.dram_tensor("scrXN0tm", [T, D], F32).ap()
    scrXN0fm = nc.dram_tensor("scrXN0fm", [D, T], F32).ap()
    scrQXfm = nc.dram_tensor("scrQXfm", [D, T], F32).ap()
    scrQWG = nc.dram_tensor("scrQWG", [D, I_FFN], F32).ap()
    scrQWU = nc.dram_tensor("scrQWU", [D, I_FFN], F32).ap()
    scrQWD = nc.dram_tensor("scrQWD", [I_FFN, D], F32).ap()
    scrQH = nc.dram_tensor("scrQH", [I_FFN, T], F32).ap()
    scrG0 = nc.dram_tensor("scrG0", [T, D], F32).ap()
    scrQ = nc.dram_tensor("scrQ", [VD, T], F32).ap()
    scrK = nc.dram_tensor("scrK", [VD, T], F32).ap()
    scrV = nc.dram_tensor("scrV", [T, NHEAD * 65], F32).ap()
    scrO = nc.dram_tensor("scrO", [T, 8 * 65], F32).ap()
    scrX1 = nc.dram_tensor("scrX1", [T, D], F32).ap()
    scrXN1tm = nc.dram_tensor("scrXN1tm", [T, D], F32).ap()
    scrXN1fm = nc.dram_tensor("scrXN1fm", [D, T], F32).ap()
    scrQI = nc.dram_tensor("scrQI", [TCH, D, 128], BF16).ap()
    scrQ1b = nc.dram_tensor("scrQ1b", [TCH, D, 128], F32).ap()
    scrWQT = nc.dram_tensor("scrWQT", [D, D], F32).ap()
    scrWQ2 = nc.dram_tensor("scrWQ2", [D, VD], F32).ap()
    scrWK2 = nc.dram_tensor("scrWK2", [D, VD], F32).ap()
    scrW1T = nc.dram_tensor("scrW1T", [D, D], BF16).ap()
    scrG1 = nc.dram_tensor("scrG1", [2 * D, D], F32).ap()

    es = ExitStack()

    def sbt(name, shape, dt=F32):
        return es.enter_context(nc.sbuf_tensor(name, shape, dt))

    ident = sbt("ident", [128, 128])
    identb = sbt("identb", [128, 128], BF16)
    ones1 = sbt("ones1", [1, 128])
    rows = sbt("rows", [1, D])
    bvrow = sbt("bvrow", [1, VD])
    bc_bout = sbt("bc_bout", [128, D])
    bc_g = sbt("bc_g", [128, D])
    bc_b = sbt("bc_b", [128, D])
    bc_b1 = sbt("bc_b1", [128, D])
    bq_sb = sbt("bq_sb", [128, 4])
    bk_sb = sbt("bk_sb", [128, 4])
    ws_sb = sbt("ws_sb", [128, 1])
    xs_all = sbt("xs_all", [128, TCH])
    stat = sbt("stat", [128, 32])
    cm8 = sbt("cm8", [128, 1])

    a1 = sbt("a1", [128, 18432])
    a2 = sbt("a2", [128, 10240])
    a3 = sbt("a3", [128, 6144])
    a4 = sbt("a4", [128, 4096])
    ab_t = sbt("ab", [128, 8192], BF16)
    u32a = sbt("u32a", [128, 1024], U32)
    u32b = sbt("u32b", [128, 1024], U32)
    u8a = sbt("u8a", [128, 1024], U8)
    AR = Arenas([(a1, 18432), (a2, 10240), (a3, 6144), (a4, 4096)])
    ARB = Arenas([(ab_t, 8192)])

    pbank = [es.enter_context(nc.psum_tensor(f"pb{i}", [128, 512], F32))
             for i in range(8)]

    sem_pool = [Ctr(es.enter_context(nc.semaphore(f"sp{i}"))) for i in range(88)]
    sem_i = [0]

    def fresh():
        c = sem_pool[sem_i[0]]
        sem_i[0] += 1
        return c

    ready = {}
    all_store_ctrs = []

    def wait_stores(e):
        for c in all_store_ctrs:
            if c.n > 0:
                e.wait_ge(c.sem, c.n)

    def wait_ready(e, *names):
        for nm in names:
            if nm in ready:
                c, v = ready[nm]
                if v > 0:
                    e.wait_ge(c.sem, v)

    def quant_fp4_scales(e, src, m_sb, r_sb, W):
        nb = W // 16
        src3 = src[:, 0:W].rearrange("p (b k) -> p b k", k=16)
        m3 = m_sb.rearrange("p (b o) -> p b o", o=1)
        e.tensor_reduce(m3, src3, axis=AX.X, op=Alu.max, apply_absolute_value=True)
        e.drain()
        e.tensor_scalar(m_sb, m_sb, float(np.float32(1.0) / np.float32(6.0)),
                        1e-12, Alu.mult, Alu.max)
        e.drain()
        e.reciprocal(r_sb, m_sb)
        e.drain()
        e.tensor_scalar(m_sb, m_sb, 0.5, None, Alu.mult)
        e.drain()

    def quant_fp4_fast(e, dst, src, u0, bA, bB, bC, m_sb, r_sb, sgnb, W):
        nb = W // 16
        src3 = src[:, 0:W].rearrange("p (b k) -> p b k", k=16)
        u3 = u0[:, 0:W].rearrange("p (b k) -> p b k", k=16)
        rb = r_sb.rearrange("p (b o) -> p b o", o=1).broadcast_to((128, nb, 16))
        mb = m_sb.rearrange("p (b o) -> p b o", o=1).broadcast_to((128, nb, 16))
        e.tensor_tensor(u3, src3, rb, op=Alu.mult)
        e.drain()
        e.scalar_tensor_tensor(u0[:, 0:W], u0[:, 0:W], -1.0, u0[:, 0:W],
                               op0=Alu.mult, op1=Alu.max)
        e.drain()
        e.tensor_scalar(bA[:, 0:W], u0[:, 0:W], 0.25, None, Alu.is_gt)
        e.tensor_scalar(bB[:, 0:W], u0[:, 0:W], 0.75, None, Alu.is_gt)
        e.drain()
        for thr, wgt in ((1.25, None), (1.75, None), (2.5, 2.0), (3.5, 2.0),
                         (5.0, 4.0), (None, None)):
            e.tensor_tensor(bA[:, 0:W], bA[:, 0:W], bB[:, 0:W], op=Alu.add)
            e.drain()
            if thr is not None:
                if wgt is None:
                    e.tensor_scalar(bB[:, 0:W], u0[:, 0:W], thr, None, Alu.is_gt)
                else:
                    e.tensor_scalar(bB[:, 0:W], u0[:, 0:W], thr, wgt, Alu.is_gt,
                                    Alu.mult)
                e.drain()
        e.tensor_tensor(bA[:, 0:W], bA[:, 0:W], sgnb[:, 0:W], op=Alu.mult)
        e.drain()
        dst3 = dst[:, 0:W].rearrange("p (b k) -> p b k", k=16)
        bA3 = bA[:, 0:W].rearrange("p (b k) -> p b k", k=16)
        e.tensor_tensor(dst3, bA3, mb, op=Alu.mult)
        e.drain()

    def quant_fp4(e, dst, src, u0, bA, bB, bC, m_sb, r_sb, sgnb, W):
        quant_fp4_scales(e, src, m_sb, r_sb, W)
        quant_fp4_fast(e, dst, src, u0, bA, bB, bC, m_sb, r_sb, sgnb, W)

    def ln_stats_vector(e, xt, cAC, ac_rstd_val):
        e.tensor_reduce(stat[:, 0:1], xt, axis=AX.X, op=Alu.add)
        e.drain()
        e.tensor_scalar(stat[:, 2:3], stat[:, 0:1], 1.0 / D, None, Alu.mult)
        e.drain()
        e.scalar_tensor_tensor(stat[:, 4:5], stat[:, 2:3], -1.0, stat[:, 2:3],
                               op0=Alu.mult, op1=Alu.mult)
        e.drain()

    AR.reset()
    dl0 = fresh()
    dl0b = fresh()
    cGP0 = fresh()
    cPE0 = fresh()
    cVE0 = fresh()
    with nc.Block() as blk:
        @blk.gpsimd
        def _(e):
            e.memset(ident[:], 0.0)
            e.drain()
            e.affine_select(out=ident[:], in_=ident[:], compare_op=Alu.not_equal,
                            fill=1.0, base=0, pattern=[[-1, 128]], channel_multiplier=1)
            e.memset(identb[:], 0.0)
            e.drain()
            e.affine_select(out=identb[:], in_=identb[:], compare_op=Alu.not_equal,
                            fill=1.0, base=0, pattern=[[-1, 128]], channel_multiplier=1)
            e.memset(ones1[:], 1.0)
            e.memset(cm8[:], EXP_SHIFT)
            e.drain()
            cGP0.mark(e)

        @blk.sync
        def _(e):
            wait_stores(e)
            dl0.inc(e.dma_start(out=bvrow[0:1, :], in_=bv_h.rearrange("(o d) -> o d", o=1)), 16)
            dl0.inc(e.dma_start(
                out=bq_sb[:, :],
                in_=bqk_h.rearrange("(a h d t) -> (h d) a t", a=4, h=2, d=64, t=2)[:, :, 0],
                allow_slow_non_contiguous=True), 16)
            dl0.inc(e.dma_start(
                out=bk_sb[:, :],
                in_=bqk_h.rearrange("(a h d t) -> (h d) a t", a=4, h=2, d=64, t=2)[:, :, 1],
                allow_slow_non_contiguous=True), 16)
            dl0.inc(e.dma_start(out=ws_sb[:, :],
                                in_=Wsc_h[0:1, :].broadcast_to((128, 1))), 16)

    bcast_jobs = [(bout_h, bc_bout), (lng_h, bc_g), (lnb_h, bc_b), (b1_h, bc_b1)]
    with nc.Block() as blk:
        @blk.sync
        def _(e):
            wait_stores(e)
            for j, (src, dst) in enumerate(bcast_jobs):
                if j > 0:
                    e.wait_ge(cPE0.sem, 2 * j)
                dl0b.inc(e.dma_start(out=rows[0:1, :], in_=src.rearrange("(o d) -> o d", o=1)), 16)

        @blk.tensor
        def _(e):
            e.wait_ge(cGP0.sem, 1)
            for j in range(4):
                e.wait_ge(dl0b.sem, 16 * (j + 1))
                if j > 0:
                    e.wait_ge(cVE0.sem, 2 * j)
                for h in range(2):
                    cPE0.inc(e.matmul(pbank[h][:, :], ones1[:, :],
                                      rows[0:1, h * 512:(h + 1) * 512],
                                      start=True, stop=True))

        @blk.vector
        def _(e):
            for j, (src, dst) in enumerate(bcast_jobs):
                e.wait_ge(cPE0.sem, 2 * (j + 1))
                e.tensor_copy(dst[:, 0:512], pbank[0][:, :])
                cVE0.mark(e)
                e.tensor_copy(dst[:, 512:1024], pbank[1][:, :])
                e.drain()
                cVE0.mark(e)

    AR.reset()
    ARB.reset()
    wqin = AR.take(D)
    wqst = AR.take(D)
    wqk_in = AR.take(D)
    wqk_q = AR.take(VD)
    wqk_k = AR.take(VD)
    dec_f = AR.take(D)
    dec_sub = AR.take(D)
    dec_mask = AR.take(D)
    dec_val = AR.take(D)
    w8b = ARB.take(D)
    w8tb = ARB.take(D)
    ld1 = fresh()
    cPE1 = fresh()
    cVE1 = fresh()
    stv1 = fresh()
    all_store_ctrs.append(stv1)
    stv1b = fresh()
    all_store_ctrs.append(stv1b)

    with nc.Block() as blk:
        @blk.sync
        def _(e):
            wait_stores(e)
            for c in range(DCH):
                if c > 0:
                    e.wait_ge(cVE1.sem, 3 * c - 1)
                ld1.inc(e.dma_start(out=wqin[:, :],
                                    in_=Wq1_h[c * 128:(c + 1) * 128, :]), 16)
                ld1.inc(e.dma_start(out=u8a[:, :],
                                    in_=W18_h[c * 128:(c + 1) * 128, :]), 16)
                ld1.inc(e.dma_start(out=wqk_in[:, :],
                                    in_=Wqk_h[c * 128:(c + 1) * 128, :]), 16)

        @blk.tensor
        def _(e):
            for c in range(DCH):
                e.wait_ge(ld1.sem, 48 * (c + 1))
                if c > 0:
                    e.wait_ge(cVE1.sem, 3 * c)
                for f in range(DCH):
                    cPE1.inc(e.transpose(pbank[f][:, 0:128],
                                         wqin[:, f * 128:(f + 1) * 128], ident[:]))
                e.wait_ge(cVE1.sem, 3 * c + 2)
                for f in range(DCH):
                    cPE1.inc(e.transpose(pbank[f][:, 0:64].bitcast(BF16),
                                         w8b[:, f * 128:(f + 1) * 128], identb[:]))

        @blk.vector
        def _(e):
            for c in range(DCH):
                base_pe = 16 * c
                e.wait_ge(ld1.sem, 48 * (c + 1))
                if c > 0:
                    e.wait_ge(stv1.sem, 16 * (3 * c))
                wv = wqk_in[:, :].rearrange("p (h d t) -> p (h d) t", h=8,
                                            d=64, t=2)
                e.tensor_copy(wqk_q[:, :], wv[:, :, 0])
                e.tensor_copy(wqk_k[:, :], wv[:, :, 1])
                e.drain()
                for f in range(DCH):
                    e.wait_ge(cPE1.sem, base_pe + f + 1)
                    e.tensor_copy(wqst[:, f * 128:(f + 1) * 128], pbank[f][:, 0:128])
                e.drain()
                cVE1.mark(e)
                e.tensor_copy(u32a[:, :], u8a[:, :])
                e.drain()
                e.tensor_scalar(u32b[:, :], u32a[:, :], 7, None,
                                Alu.logical_shift_right)
                e.drain()
                e.tensor_copy(dec_f[:, :], u32b[:, :])
                e.tensor_scalar(u32a[:, :], u32a[:, :], 127, None, Alu.bitwise_and)
                e.drain()
                e.tensor_copy(dec_sub[:, :], u32a[:, :])
                e.tensor_scalar(u32b[:, :], u32a[:, :], 8, None, Alu.is_lt)
                e.drain()
                e.tensor_scalar(u32a[:, :], u32a[:, :], 960.0, 1048576.0, Alu.add,
                                Alu.mult)
                e.drain()
                e.tensor_scalar(dec_sub[:, :], dec_sub[:, :], float(2.0 ** -9),
                                None, Alu.mult)
                e.tensor_scalar(dec_f[:, :], dec_f[:, :], -2.0, 1.0, Alu.mult, Alu.add)
                e.drain()
                e.select(dec_val[:, :], u32b[:, :], dec_sub[:, :],
                         u32a[:, :].bitcast(F32), add_drain=True)
                e.drain()
                e.tensor_tensor(w8b[:, :], dec_val[:, :], dec_f[:, :], op=Alu.mult)
                e.drain()
                cVE1.mark(e)
                if c > 0:
                    e.wait_ge(stv1b.sem, 16 * c)
                for f in range(DCH):
                    e.wait_ge(cPE1.sem, base_pe + 8 + f + 1)
                    e.tensor_copy(w8tb[:, f * 128:(f + 1) * 128],
                                  pbank[f][:, 0:64].bitcast(BF16))
                e.drain()
                cVE1.mark(e)

        @blk.gpsimd
        def _(e):
            for c in range(DCH):
                e.wait_ge(cVE1.sem, 3 * c + 1)
                stv1.inc(e.dma_start(
                    out=scrWQT[:, c * 128:(c + 1) * 128].rearrange(
                        "(f p) o -> p f o", p=128),
                    in_=wqst[:, :].rearrange("p (f o) -> p f o", o=128)), 16)
                stv1.inc(e.dma_start(out=scrWQ2[c * 128:(c + 1) * 128, :],
                                     in_=wqk_q[:, :]), 16)
                stv1.inc(e.dma_start(out=scrWK2[c * 128:(c + 1) * 128, :],
                                     in_=wqk_k[:, :]), 16)
                e.wait_ge(cVE1.sem, 3 * c + 3)
                stv1b.inc(e.dma_start(
                    out=scrW1T[:, c * 128:(c + 1) * 128].rearrange(
                        "(f p) o -> p f o", p=128),
                    in_=w8tb[:, :].rearrange("p (f o) -> p f o", o=128)), 16)

    ready["WQT"] = (stv1, stv1.n)
    ready["W1T"] = (stv1b, stv1b.n)


    AR.reset()
    xin = [AR.take(D), AR.take(D)]
    xnt = AR.take(D)
    sq1 = AR.take(D)
    tst1 = AR.take(D)
    ld3 = [fresh(), fresh()]
    cPE3 = fresh()
    cVE3 = fresh()
    cAC3 = fresh()
    stv3 = fresh()
    all_store_ctrs.append(stv3)
    sta3 = fresh()
    all_store_ctrs.append(sta3)

    with nc.Block() as blk:
        @blk.sync
        def _(e):
            wait_stores(e)
            for c in range(TCH):
                sl = c % 2
                if c >= 2:
                    e.wait_ge(cAC3.sem, 2 * (c - 2) + 2)
                ld3[sl].inc(e.dma_start(out=xin[sl][:, :],
                                        in_=x_in[c * 128:(c + 1) * 128, :]), 16)

        @blk.vector
        def _(e):
            for c in range(TCH):
                sl = c % 2
                e.wait_ge(ld3[sl].sem, 16 * (c // 2 + 1))
                xt = xin[sl]
                e.tensor_reduce(stat[:, 0:1], xt[:, :], axis=AX.X, op=Alu.add)
                e.tensor_tensor(sq1[:, :], xt[:, :], xt[:, :], op=Alu.mult)
                e.drain()
                e.tensor_reduce(stat[:, 1:2], sq1[:, :], axis=AX.X, op=Alu.add)
                e.drain()
                e.tensor_scalar(stat[:, 2:3], stat[:, 0:1], 1.0 / D, None, Alu.mult)
                e.tensor_scalar(stat[:, 3:4], stat[:, 1:2], 1.0 / D, None, Alu.mult)
                e.drain()
                e.scalar_tensor_tensor(stat[:, 4:5], stat[:, 2:3], -1.0, stat[:, 2:3],
                                       op0=Alu.mult, op1=Alu.mult)
                e.drain()
                e.tensor_tensor(stat[:, 5:6], stat[:, 3:4], stat[:, 4:5], op=Alu.add)
                e.drain()
                e.tensor_scalar(stat[:, 13:14], stat[:, 5:6], LN_EPS, None, Alu.add)
                e.drain()
                e.reciprocal(stat[:, 14:15], stat[:, 13:14])
                e.drain()
                cVE3.mark(e)
                e.wait_ge(cAC3.sem, 2 * c + 1)
                e.scalar_tensor_tensor(stat[:, 7:8], stat[:, 2:3], -1.0, stat[:, 6:7],
                                       op0=Alu.mult, op1=Alu.mult)
                e.drain()
                cVE3.mark(e)
                if c > 0:
                    e.wait_ge(stv3.sem, 16 * c)
                for f in range(DCH):
                    e.wait_ge(cPE3.sem, 8 * c + f + 1)
                    e.tensor_copy(tst1[:, f * 128:(f + 1) * 128], pbank[f][:, 0:128])
                e.drain()
                cVE3.mark(e)

        @blk.scalar
        def _(e):
            for c in range(TCH):
                sl = c % 2
                e.wait_ge(cVE3.sem, 3 * c + 1)
                e.activation(stat[:, 6:7], stat[:, 14:15], AF.Sqrt)
                e.drain()
                cAC3.mark(e)
                e.wait_ge(cVE3.sem, 3 * c + 2)
                if c > 0:
                    e.wait_ge(sta3.sem, 16 * c)
                e.activation(xnt[:, :], xin[sl][:, :], AF.Identity,
                             bias=stat[:, 7:8], scale=stat[:, 6:7])
                e.drain()
                cAC3.mark(e)
                sta3.inc(e.dma_start(out=scrXN0tm[c * 128:(c + 1) * 128, :],
                                     in_=xnt[:, :]), 16)

        @blk.tensor
        def _(e):
            for c in range(TCH):
                e.wait_ge(cAC3.sem, 2 * c + 2)
                if c > 0:
                    e.wait_ge(cVE3.sem, 3 * c)
                for f in range(DCH):
                    cPE3.inc(e.transpose(pbank[f][:, 0:128],
                                         xnt[:, f * 128:(f + 1) * 128], ident[:]))

        @blk.gpsimd
        def _(e):
            for c in range(TCH):
                e.wait_ge(cVE3.sem, 3 * c + 3)
                stv3.inc(e.dma_start(
                    out=scrXN0fm[:, c * 128:(c + 1) * 128].rearrange(
                        "(f p) t -> p f t", p=128),
                    in_=tst1[:, :].rearrange("p (f t) -> p f t", t=128)), 16)

    ready["XN0tm"] = (sta3, sta3.n)
    ready["XN0fm"] = (stv3, stv3.n)

    AR.reset()
    wq_sb = [AR.take(VD) for _ in range(DCH)]
    wk_sb = [AR.take(VD) for _ in range(DCH)]
    wv_sb = [AR.take(VD) for _ in range(DCH)]
    xnf = [AR.take(DCH * 512), AR.take(DCH * 512)]
    qkst = [AR.take(512), AR.take(512)]
    vst = AR.take(NHEAD * 65)
    ldw6 = fresh()
    ldx6 = [fresh(), fresh()]
    cPE6 = fresh()
    cVE6 = fresh()
    cAC6 = fresh()
    stv6 = fresh()
    all_store_ctrs.append(stv6)
    sta6 = fresh()
    all_store_ctrs.append(sta6)

    with nc.Block() as blk:
        @blk.sync
        def _(e):
            wait_stores(e)
            wait_ready(e, "XN0fm", "WQT")
            for f in range(DCH):
                ldw6.inc(e.dma_start(out=wq_sb[f][:, :],
                                     in_=scrWQ2[f * 128:(f + 1) * 128, :]), 16)
                ldw6.inc(e.dma_start(out=wk_sb[f][:, :],
                                     in_=scrWK2[f * 128:(f + 1) * 128, :]), 16)
                ldw6.inc(e.dma_start(out=wv_sb[f][:, :],
                                     in_=Wv_h[f * 128:(f + 1) * 128, :]), 16)
            for ts in range(8):
                sl = ts % 2
                if ts >= 2:
                    e.wait_ge(cPE6.sem, 12 * (ts - 1))
                ldx6[sl].inc(e.dma_start(
                    out=xnf[sl][:, :],
                    in_=scrXN0fm[:, ts * 512:(ts + 1) * 512].rearrange(
                        "(f p) t -> p f t", p=128)), 16)

        @blk.tensor
        def _(e):
            e.wait_ge(dl0.sem, dl0.n)
            e.wait_ge(ldw6.sem, 16 * 3 * DCH)
            for ts in range(8):
                sl = ts % 2
                e.wait_ge(ldx6[sl].sem, 16 * (ts // 2 + 1))
                for qc in range(4):
                    for kind in range(2):
                        grp = 8 * ts + 2 * qc + kind
                        b = pbank[grp % 6]
                        if grp >= 6:
                            e.wait_ge(cAC6.sem, grp - 5)
                        wsb = wq_sb if kind == 0 else wk_sb
                        for f in range(DCH):
                            mm = e.matmul(b[:, :],
                                          wsb[f][:, qc * 128:(qc + 1) * 128],
                                          xnf[sl][:, f * 512:(f + 1) * 512],
                                          start=(f == 0), stop=(f == DCH - 1))
                        cPE6.inc(mm)
                for tc in range(4):
                    b = pbank[6 + tc % 2]
                    grp_v = 4 * ts + tc
                    if grp_v >= 2:
                        e.wait_ge(cVE6.sem, grp_v - 1)
                    for f in range(DCH):
                        e.matmul(b[:, :],
                                 xnf[sl][:, f * 512 + tc * 128:
                                         f * 512 + (tc + 1) * 128],
                                 wv_sb[f][:, :],
                                 start=(f == 0), stop=False)
                    mm = e.matmul(b[:, :], ones1[:, :], bvrow[0:1, :],
                                  start=False, stop=True)
                    cPE6.inc(mm)

        @blk.scalar
        def _(e):
            e.wait_ge(dl0.sem, dl0.n)
            for ts in range(8):
                for qc in range(4):
                    for kind in range(2):
                        grp = 8 * ts + 2 * qc + kind
                        b = pbank[grp % 6]
                        sl6 = grp % 2
                        if grp >= 2:
                            e.wait_ge(sta6.sem, 16 * grp)
                        e.wait_ge(cPE6.sem, 12 * ts + 2 * qc + kind + 1)
                        bias = (bq_sb if kind == 0 else bk_sb)[:, qc:qc + 1]
                        e.activation(qkst[sl6][:, :], b[:, :], AF.Identity,
                                     bias=bias, scale=1.0)
                        e.drain()
                        dst = (scrQ if kind == 0 else scrK)
                        sta6.inc(e.dma_start(
                            out=dst[qc * 128:(qc + 1) * 128,
                                    ts * 512:(ts + 1) * 512],
                            in_=qkst[sl6][:, :]), 16)
                        cAC6.mark(e)

        @blk.vector
        def _(e):
            on = vst.rearrange("p (h j) -> p h j", j=65)
            e.wait_ge(ldw6.sem, 16 * 3 * DCH)
            e.memset(on[:, :, 64:65], 1.0)
            e.drain()
            for ts in range(8):
                for tc in range(4):
                    b = pbank[6 + tc % 2]
                    gv = 4 * ts + tc
                    if gv >= 1:
                        e.wait_ge(stv6.sem, 16 * gv)
                    e.wait_ge(cPE6.sem, 12 * ts + 8 + tc + 1)
                    e.tensor_copy(on[:, :, 0:64],
                                  b[:, :].rearrange("p (h j) -> p h j", j=64))
                    e.drain()
                    cVE6.mark(e)

        @blk.gpsimd
        def _(e):
            for gv in range(32):
                e.wait_ge(cVE6.sem, gv + 1)
                ts, tc = gv // 4, gv % 4
                stv6.inc(e.dma_start(
                    out=scrV[ts * 512 + tc * 128: ts * 512 + (tc + 1) * 128, :],
                    in_=vst[:, :]), 16)

    ready["Q"] = (sta6, sta6.n)
    ready["K"] = (sta6, sta6.n)
    ready["V"] = (stv6, stv6.n)

    AR.reset()
    ARB.reset()
    pTa = AR.take(16 * 512)
    pTb1 = AR.take(8 * 512)
    pTb2 = AR.take(8 * 512)
    qcb = [AR.take(2048), AR.take(2048)]
    kcb = [AR.take(2048), AR.take(2048)]
    vtb = AR.take(16 * 130)
    ost = AR.take(520)
    oT_sb = AR.take(1024)

    def pT_sl(hh, kt, lo, hi):
        if hh == 0:
            return pTa[:, kt * 512 + lo:kt * 512 + hi]
        base = pTb1 if kt < 8 else pTb2
        return base[:, (kt % 8) * 512 + lo:(kt % 8) * 512 + hi]

    bgwin = [AR.take(1408), AR.take(1408)]
    bgqv = [AR.take(1408) for _ in range(3)]
    bgtst = AR.take(1408)
    bgmm = AR.take(88)
    bgrr = AR.take(88)
    bgA = ARB.take(1408)
    bgB = ARB.take(1408)
    bgsg = [ARB.take(1408), ARB.take(1408)]

    ld7 = [fresh(), fresh()]
    cPE7 = fresh()
    cAC7 = fresh()
    cAC7b = fresh()
    cTR = fresh()
    cTE = fresh()
    stv7 = fresh()
    all_store_ctrs.append(stv7)
    ldb = [fresh(), fresh()]
    cBQ = fresh()
    sbst = fresh()
    all_store_ctrs.append(sbst)

    ajobs = [(s, cpair) for s in range(2) for cpair in range(4)]
    PE_J = 136
    AC_J = 128

    bgjobs = []
    for c in range(ICH):
        bgjobs.append(("g", 1024, Wg_h[c * 128:(c + 1) * 128, :],
                       scrQWG[:, c * 128:(c + 1) * 128]))
    for c in range(ICH):
        bgjobs.append(("u", 1024, Wu_h[c * 128:(c + 1) * 128, :],
                       scrQWU[:, c * 128:(c + 1) * 128]))
    for c in range(TCH):
        bgjobs.append(("qx", 1024, scrXN0tm[c * 128:(c + 1) * 128, :],
                       scrQXfm[:, c * 128:(c + 1) * 128]))
    for c in range(DCH):
        for hh2 in range(2):
            bgjobs.append(("d", 1408,
                           Wd_h[c * 128:(c + 1) * 128,
                                hh2 * 1408:(hh2 + 1) * 1408],
                           scrQWD[hh2 * 1408:(hh2 + 1) * 1408,
                                  c * 128:(c + 1) * 128]))
    NBG = len(bgjobs)
    BG_RING = True
    NUNITS = 32
    ucut = [NBG * (u + 1) // NUNITS for u in range(NUNITS)]
    ucut = [0] + ucut

    def unit_jobs(u):
        if u < 0 or u >= NUNITS:
            return range(0, 0)
        return range(ucut[u], ucut[u + 1])

    ring_base = {}
    ring_last_bg = {}
    o_pos = {}
    _rn = 0
    for _u in range(NUNITS):
        ring_base[_u] = _rn
        for _k in (unit_jobs(_u - 1) if BG_RING else []):
            _rn += bgjobs[_k][1] // 128
            ring_last_bg[_k] = _rn - 1
        for _qq in range(4):
            for _hh in range(2):
                o_pos[(_u, _qq, _hh)] = _rn
                _rn += 1
    ring_base[NUNITS] = _rn
    for _k in (unit_jobs(NUNITS - 1) if BG_RING else []):
        _rn += bgjobs[_k][1] // 128
        ring_last_bg[_k] = _rn - 1

    NRING = _rn

    def rgrp(r):
        return min(((r // 4) + 1) * 4, NRING)

    def rslot(r):
        sslot = r % 8
        return pbank[6 + sslot // 4], (sslot % 4) * 128

    with nc.Block() as blk:
        @blk.sync
        def _(e):
            wait_stores(e)
            wait_ready(e, "Q", "K", "V", "XN0tm")
            for ji, (s, cp) in enumerate(ajobs):
                sl = ji % 2
                if ji >= 2:
                    e.wait_ge(cPE7.sem, PE_J * (ji - 1))
                ld7[sl].inc(e.dma_start(
                    out=qcb[sl][:, :],
                    in_=scrQ[cp * 128:(cp + 1) * 128,
                             s * 2048:(s + 1) * 2048]), 16)
                ld7[sl].inc(e.dma_start(
                    out=kcb[sl][:, :],
                    in_=scrK[cp * 128:(cp + 1) * 128,
                             s * 2048:(s + 1) * 2048]), 16)
                if ji >= 1:
                    e.wait_ge(cPE7.sem, PE_J * ji)
                ld7[sl].inc(e.dma_start(
                    out=vtb[:, :],
                    in_=scrV[s * 2048:(s + 1) * 2048,
                             cp * 130:(cp + 1) * 130].rearrange(
                        "(kt p) j -> p kt j", p=128)), 16)
                for u in range(4 * ji, 4 * ji + 4):
                    for k in unit_jobs(u):
                        slb = k % 2
                        if k >= 2:
                            e.wait_ge(cBQ.sem, k - 1)
                        W = bgjobs[k][1]
                        ldb[slb].inc(e.dma_start(out=bgwin[slb][:, 0:W],
                                                 in_=bgjobs[k][2]), 16)

        @blk.tensor
        def _(e):
            for ji, (s, cp) in enumerate(ajobs):
                sl = ji % 2
                e.wait_ge(ld7[sl].sem, 48 * (ji // 2 + 1))
                pe_base = PE_J * ji
                ac_base = AC_J * ji
                for qs in range(4):
                    u = 4 * ji + qs
                    for kt in range(16):
                        for hh in range(2):
                            b = pbank[2 * (kt % 2) + hh]
                            grp = 32 * qs + 2 * kt + hh
                            if grp >= 4 or ji > 0:
                                e.wait_ge(cAC7.sem, ac_base + grp - 3)
                            lo = 64 * hh
                            mm = e.matmul(b[:, :],
                                          kcb[sl][lo:lo + 64,
                                                  kt * 128:(kt + 1) * 128],
                                          qcb[sl][lo:lo + 64,
                                                  qs * 512:(qs + 1) * 512],
                                          start=True, stop=True)
                            cPE7.inc(mm)
                    e.wait_ge(cAC7.sem, ac_base + 32 * (qs + 1))
                    for hh in range(2):
                        b = pbank[4 + hh]
                        if u >= 1:
                            e.wait_ge(cAC7b.sem, 2 * (u - 1) + hh + 1)
                        for kt in range(16):
                            mm = e.matmul(
                                b[0:65, 0:512],
                                vtb[:, kt * 130 + hh * 65:
                                    kt * 130 + hh * 65 + 65],
                                pT_sl(hh, kt, 0, 512),
                                start=(kt == 0), stop=(kt == 15))
                        cPE7.inc(mm)
                    r = ring_base[u]
                    for k in (unit_jobs(u - 1) if BG_RING else []):
                        e.wait_ge(cBQ.sem, k + 1)
                        for f in range(bgjobs[k][1] // 128):
                            if r % 4 == 0 and r >= 8:
                                e.wait_ge(cTE.sem, r - 4)
                            bank, col = rslot(r)
                            cTR.inc(e.transpose(
                                bank[:, col:col + 128],
                                bgqv[k % 3][:, f * 128:(f + 1) * 128],
                                ident[:]))
                            r += 1
                    for qq in range(4):
                        for hh in range(2):
                            if qq == 0:
                                e.wait_ge(cAC7b.sem, 2 * u + hh + 1)
                            if r % 4 == 0 and r >= 8:
                                e.wait_ge(cTE.sem, r - 4)
                            bank, col = rslot(r)
                            cTR.inc(e.transpose(
                                bank[:, col:col + 65],
                                oT_sb[0:65, hh * 512 + qq * 128:
                                      hh * 512 + (qq + 1) * 128],
                                ident[0:65, 0:65]))
                            r += 1
            r = ring_base[NUNITS]
            for k in (unit_jobs(NUNITS - 1) if BG_RING else []):
                e.wait_ge(cBQ.sem, k + 1)
                for f in range(bgjobs[k][1] // 128):
                    if r % 4 == 0 and r >= 8:
                        e.wait_ge(cTE.sem, r - 4)
                    bank, col = rslot(r)
                    cTR.inc(e.transpose(
                        bank[:, col:col + 128],
                        bgqv[k % 3][:, f * 128:(f + 1) * 128], ident[:]))
                    r += 1

        @blk.scalar
        def _(e):
            for ji, (s, cp) in enumerate(ajobs):
                pe_base = PE_J * ji
                for qs in range(4):
                    u = 4 * ji + qs
                    for kt in range(16):
                        for hh in range(2):
                            b = pbank[2 * (kt % 2) + hh]
                            e.wait_ge(cPE7.sem,
                                      pe_base + 34 * qs + 2 * kt + hh + 1)
                            e.activation(pT_sl(hh, kt, 0, 512), b[:, :],
                                         AF.Exp, bias=cm8[:, 0:1], scale=0.125)
                            e.drain()
                            cAC7.mark(e)
                    if u >= 1:
                        e.wait_ge(cTR.sem, o_pos[(u - 1, 3, 1)] + 1)
                    for hh in range(2):
                        e.wait_ge(cPE7.sem, pe_base + 34 * qs + 32 + hh + 1)
                        e.activation(oT_sb[0:65, hh * 512:(hh + 1) * 512],
                                     pbank[4 + hh][0:65, 0:512], AF.Identity,
                                     bias=0.0, scale=1.0)
                        e.drain()
                        cAC7b.mark(e)

        @blk.vector
        def _(e):
            for ji, (s, cp) in enumerate(ajobs):
                for qs in range(4):
                    u = 4 * ji + qs
                    r = ring_base[u]
                    for k in (unit_jobs(u - 1) if BG_RING else []):
                        if k > 0:
                            e.wait_ge(sbst.sem, 16 * k)
                        for f in range(bgjobs[k][1] // 128):
                            e.wait_ge(cTR.sem, rgrp(r))
                            bank, col = rslot(r)
                            e.tensor_copy(bgtst[:, f * 128:(f + 1) * 128],
                                          bank[:, col:col + 128])
                            e.drain()
                            cTE.mark(e)
                            r += 1
                    for k in unit_jobs(u):
                        slb = k % 2
                        W = bgjobs[k][1]
                        e.wait_ge(ldb[slb].sem, 16 * (k // 2 + 1))
                        if k >= 3 and BG_RING:
                            e.wait_ge(cTR.sem, ring_last_bg[k - 3] + 1)
                        e.tensor_scalar(bgsg[0][:, 0:W], bgwin[slb][:, 0:W],
                                        0.0, 2.0, Alu.is_gt, Alu.mult)
                        e.drain()
                        e.tensor_scalar(bgsg[0][:, 0:W], bgsg[0][:, 0:W], 1.0,
                                        None, Alu.subtract)
                        e.drain()
                        quant_fp4_scales(e, bgwin[slb][:, 0:W],
                                         bgmm[:, 0:W // 16], bgrr[:, 0:W // 16],
                                         W)
                        quant_fp4_fast(e, bgqv[k % 3], bgwin[slb],
                                       bgwin[slb], bgA, bgB, None,
                                       bgmm[:, 0:W // 16],
                                       bgrr[:, 0:W // 16], bgsg[0], W)
                        cBQ.mark(e)
                    for qq in range(4):
                        for hh in range(2):
                            if qq == 0 and hh == 0 and u > 0:
                                e.wait_ge(stv7.sem, 64 * u)
                            e.wait_ge(cTR.sem, rgrp(r))
                            bank, col = rslot(r)
                            e.tensor_copy(
                                ost[:, qq * 130 + hh * 65:
                                    qq * 130 + hh * 65 + 65],
                                bank[:, col:col + 65])
                            e.drain()
                            cTE.mark(e)
                            r += 1
            r = ring_base[NUNITS]
            for k in (unit_jobs(NUNITS - 1) if BG_RING else []):
                if k > 0:
                    e.wait_ge(sbst.sem, 16 * k)
                for f in range(bgjobs[k][1] // 128):
                    e.wait_ge(cTR.sem, rgrp(r))
                    bank, col = rslot(r)
                    e.tensor_copy(bgtst[:, f * 128:(f + 1) * 128],
                                  bank[:, col:col + 128])
                    e.drain()
                    cTE.mark(e)
                    r += 1

        @blk.gpsimd
        def _(e):
            for ji, (s, cp) in enumerate(ajobs):
                for qs in range(4):
                    u = 4 * ji + qs
                    for k in (unit_jobs(u - 1) if BG_RING else []):
                        e.wait_ge(cTE.sem, ring_last_bg[k] + 1)
                        W = bgjobs[k][1]
                        sbst.inc(e.dma_start(
                            out=bgjobs[k][3].rearrange("(f p) o -> p f o", p=128),
                            in_=bgtst[:, 0:W].rearrange("p (f o) -> p f o",
                                                        o=128)), 16)
                    for qq in range(4):
                        e.wait_ge(cTE.sem, o_pos[(u, qq, 1)] + 1)
                        tok0 = s * 2048 + qs * 512 + qq * 128
                        stv7.inc(e.dma_start(
                            out=scrO[tok0:tok0 + 128,
                                     cp * 130:(cp + 1) * 130],
                            in_=ost[:, qq * 130:(qq + 1) * 130]), 16)
            for k in (unit_jobs(NUNITS - 1) if BG_RING else []):
                e.wait_ge(cTE.sem, ring_last_bg[k] + 1)
                W = bgjobs[k][1]
                sbst.inc(e.dma_start(
                    out=bgjobs[k][3].rearrange("(f p) o -> p f o", p=128),
                    in_=bgtst[:, 0:W].rearrange("p (f o) -> p f o",
                                                o=128)), 16)

    ready["O"] = (stv7, stv7.n)
    ready["QWG"] = (sbst, sbst.n)
    ready["QWU"] = (sbst, sbst.n)
    ready["QWD"] = (sbst, sbst.n)
    ready["QXfm"] = (sbst, sbst.n)

    AR.reset()
    isl_sizes = [512] * 5 + [256]
    isl_off = [0, 512, 1024, 1536, 2048, 2560]
    wgu = [AR.take(DCH * 512), AR.take(DCH * 512)]
    wuu = [AR.take(DCH * 512), AR.take(DCH * 512)]
    qxb = [AR.take(D), AR.take(D)]
    hti = AR.take(512)
    tmg = AR.take(512)
    tA2 = AR.take(512)
    qh2 = [AR.take(512), AR.take(512)]
    tst2 = AR.take(512)
    ARB.reset()
    bA2 = ARB.take(512)
    bB2 = ARB.take(512)
    bC2 = ARB.take(512)
    sg2b = ARB.take(512)
    mm2 = AR.take(32)
    rr2 = AR.take(32)
    ldw = [fresh(), fresh()]
    ldq = [fresh(), fresh()]
    cPE4 = fresh()
    cPT4 = fresh()
    cVE4 = fresh()
    cAC4 = fresh()
    stv4 = fresh()
    all_store_ctrs.append(stv4)
    NJ2 = len(isl_sizes) * TCH

    def _isz(j):
        return isl_sizes[j // TCH]

    with nc.Block() as blk:
        @blk.sync
        def _(e):
            wait_stores(e)
            wait_ready(e, "QWG", "QWU", "QXfm")
            j = 0
            for i, isz in enumerate(isl_sizes):
                slw = i % 2
                if i >= 2:
                    e.wait_ge(cPE4.sem, 32 * (i - 1))
                ldw[slw].inc(e.dma_start(
                    out=wgu[slw][:, 0:DCH * isz],
                    in_=scrQWG[:, isl_off[i]:isl_off[i] + isz].rearrange(
                        "(f p) i -> p f i", p=128)), 16)
                ldw[slw].inc(e.dma_start(
                    out=wuu[slw][:, 0:DCH * isz],
                    in_=scrQWU[:, isl_off[i]:isl_off[i] + isz].rearrange(
                        "(f p) i -> p f i", p=128)), 16)
                for c in range(TCH):
                    slq = j % 2
                    if j >= 2:
                        e.wait_ge(cPE4.sem, j - 1)
                    ldq[slq].inc(e.dma_start(
                        out=qxb[slq][:, :],
                        in_=scrQXfm[:, c * 128:(c + 1) * 128].rearrange(
                            "(f p) t -> p f t", p=128)), 16)
                    j += 1

        @blk.tensor
        def _(e):
            for j in range(NJ2 + 2):
                if j < NJ2:
                    i, c = j // TCH, j % TCH
                    isz = isl_sizes[i]
                    slw, slq = i % 2, j % 2
                    e.wait_ge(ldw[slw].sem, 32 * (i // 2 + 1))
                    e.wait_ge(ldq[slq].sem, 16 * (j // 2 + 1))
                    if j >= 2:
                        e.wait_ge(cVE4.sem, 3 * (j - 2) + 1)
                    bg = pbank[2 * (j % 2)]
                    bu = pbank[2 * (j % 2) + 1]
                    for f in range(DCH):
                        e.matmul(bg[:, 0:isz], qxb[slq][:, f * 128:(f + 1) * 128],
                                 wgu[slw][:, f * isz:(f + 1) * isz],
                                 start=(f == 0), stop=(f == DCH - 1))
                    for f in range(DCH):
                        mm = e.matmul(bu[:, 0:isz],
                                      qxb[slq][:, f * 128:(f + 1) * 128],
                                      wuu[slw][:, f * isz:(f + 1) * isz],
                                      start=(f == 0), stop=(f == DCH - 1))
                    cPE4.inc(mm)
                if j >= 2:
                    k = j - 2
                    ksz = _isz(k)
                    e.wait_ge(cVE4.sem, 3 * k + 2)
                    if k >= 1:
                        e.wait_ge(cVE4.sem, 3 * k)
                    for f in range(ksz // 128):
                        mm = e.transpose(pbank[4 + f % 4][:, 0:128],
                                         qh2[k % 2][:, f * 128:(f + 1) * 128],
                                         ident[:])
                    cPT4.inc(mm)

        @blk.scalar
        def _(e):
            for j in range(NJ2):
                isz = _isz(j)
                bg = pbank[2 * (j % 2)]
                e.wait_ge(cPE4.sem, j + 1)
                if j >= 1:
                    e.wait_ge(cVE4.sem, 3 * (j - 1) + 1)
                e.activation(tmg[:, 0:isz], bg[:, 0:isz], AF.Silu)
                e.drain()
                cAC4.mark(e)

        @blk.vector
        def _(e):
            for j in range(NJ2):
                isz = _isz(j)
                bu = pbank[2 * (j % 2) + 1]
                e.wait_ge(cAC4.sem, j + 1)
                e.tensor_tensor(hti[:, 0:isz], tmg[:, 0:isz], bu[:, 0:isz],
                                op=Alu.mult)
                e.drain()
                cVE4.mark(e)
                e.tensor_scalar(sg2b[:, 0:isz], hti[:, 0:isz], 0.0, 2.0,
                                Alu.is_gt, Alu.mult)
                e.drain()
                e.tensor_scalar(sg2b[:, 0:isz], sg2b[:, 0:isz], 1.0, None,
                                Alu.subtract)
                e.drain()
                if j >= 2:
                    e.wait_ge(cPT4.sem, j - 1)
                quant_fp4(e, qh2[j % 2], hti, tA2, bA2, bB2, bC2,
                          mm2[:, 0:isz // 16], rr2[:, 0:isz // 16], sg2b, isz)
                cVE4.mark(e)
                e.wait_ge(cPT4.sem, j + 1)
                if j > 0:
                    e.wait_ge(stv4.sem, 16 * j)
                for f in range(isz // 128):
                    e.tensor_copy(tst2[:, f * 128:(f + 1) * 128],
                                  pbank[4 + f % 4][:, 0:128])
                e.drain()
                cVE4.mark(e)

        @blk.gpsimd
        def _(e):
            for j in range(NJ2):
                i = j // TCH
                c = j % TCH
                isz = isl_sizes[i]
                e.wait_ge(cVE4.sem, 3 * j + 3)
                stv4.inc(e.dma_start(
                    out=scrQH[isl_off[i]:isl_off[i] + isz,
                              c * 128:(c + 1) * 128].rearrange(
                        "(f p) t -> p f t", p=128),
                    in_=tst2[:, 0:isz].rearrange("p (f t) -> p f t", t=128)), 16)

    ready["QH"] = (stv4, stv4.n)

    AR.reset()
    wd_tiles = [AR.take(D) for _ in range(ICH)]
    qh_sl_a = AR.take(11 * 512)
    qh_sl_b = AR.take(11 * 512)
    g0st = [AR.take(D), AR.take(D)]

    def qh_chunk(ic, lo, hi):
        base = qh_sl_a if ic < 11 else qh_sl_b
        return base[:, (ic % 11) * 512 + lo:(ic % 11) * 512 + hi]
    ldwd = fresh()
    ldqh = fresh()
    cPE5 = fresh()
    cAC5 = fresh()
    sta5 = fresh()
    all_store_ctrs.append(sta5)

    with nc.Block() as blk:
        @blk.sync
        def _(e):
            wait_stores(e)
            wait_ready(e, "QWD", "QH")
            for ic in range(ICH):
                ldwd.inc(e.dma_start(out=wd_tiles[ic][:, :],
                                     in_=scrQWD[ic * 128:(ic + 1) * 128, :]), 16)
            for tt in range(8):
                if tt > 0:
                    e.wait_ge(cPE5.sem, 8 * tt)
                ldqh.inc(e.dma_start(
                    out=qh_sl_a[:, :],
                    in_=scrQH[0:11 * 128, tt * 512:(tt + 1) * 512].rearrange(
                        "(f p) t -> p f t", p=128)), 16)
                ldqh.inc(e.dma_start(
                    out=qh_sl_b[:, :],
                    in_=scrQH[11 * 128:22 * 128,
                              tt * 512:(tt + 1) * 512].rearrange(
                        "(f p) t -> p f t", p=128)), 16)

        @blk.tensor
        def _(e):
            e.wait_ge(ldwd.sem, 16 * ICH)
            for tt in range(8):
                e.wait_ge(ldqh.sem, 32 * (tt + 1))
                if tt > 0:
                    e.wait_ge(cAC5.sem, 12 * tt)
                for tc in range(4):
                    for hh in range(2):
                        b = pbank[2 * tc + hh]
                        for ic in range(ICH):
                            mm = e.matmul(b[:, :],
                                          qh_chunk(ic, tc * 128, (tc + 1) * 128),
                                          wd_tiles[ic][:, hh * 512:(hh + 1) * 512],
                                          start=(ic == 0), stop=(ic == ICH - 1))
                        cPE5.inc(mm)

        @blk.scalar
        def _(e):
            for tt in range(8):
                for tc in range(4):
                    sl = tc % 2
                    g04 = 4 * tt + tc
                    if g04 >= 2:
                        e.wait_ge(sta5.sem, 16 * g04)
                    for hh in range(2):
                        e.wait_ge(cPE5.sem, 8 * tt + 2 * tc + hh + 1)
                        e.activation(g0st[sl][:, hh * 512:(hh + 1) * 512],
                                     pbank[2 * tc + hh][:, :], AF.Silu)
                        e.drain()
                        cAC5.mark(e)
                    sta5.inc(e.dma_start(
                        out=scrG0[tt * 512 + tc * 128: tt * 512 + (tc + 1) * 128, :],
                        in_=g0st[sl][:, :]), 16)
                    cAC5.mark(e)

    ready["G0"] = (sta5, sta5.n)

    AR.reset()
    wout_sb = [AR.take(D) for _ in range(4)]
    oin = [AR.take(520), AR.take(520)]
    onrm = AR.take(VD)
    xn0in = [AR.take(D), AR.take(D)]
    g0in = [AR.take(D), AR.take(D)]
    x0in = [AR.take(D), AR.take(D)]
    ofm = AR.take(VD)
    t0 = AR.take(D)
    sqs = AR.take(D)
    z0 = AR.take(D)
    x1o = [AR.take(D), AR.take(D)]
    ld8 = [fresh(), fresh()]
    ldw8 = fresh()
    cPE8 = fresh()
    cVE8 = fresh()
    cAC8 = fresh()
    sta8 = fresh()
    all_store_ctrs.append(sta8)

    with nc.Block() as blk:
        @blk.sync
        def _(e):
            wait_stores(e)
            wait_ready(e, "O", "XN0tm", "G0")
            for f in range(4):
                ldw8.inc(e.dma_start(out=wout_sb[f][:, :],
                                     in_=Wout_h[f * 128:(f + 1) * 128, :]), 16)
            for c in range(TCH):
                sl = c % 2
                if c >= 2:
                    e.wait_ge(cVE8.sem, 6 * (c - 1))
                ld8[sl].inc(e.dma_start(out=oin[sl][:, :],
                                        in_=scrO[c * 128:(c + 1) * 128, :]), 16)
                ld8[sl].inc(e.dma_start(out=xn0in[sl][:, :],
                                        in_=scrXN0tm[c * 128:(c + 1) * 128, :]), 16)
                ld8[sl].inc(e.dma_start(out=g0in[sl][:, :],
                                        in_=scrG0[c * 128:(c + 1) * 128, :]), 16)
                ld8[sl].inc(e.dma_start(out=x0in[sl][:, :],
                                        in_=x_in[c * 128:(c + 1) * 128, :]), 16)

        @blk.tensor
        def _(e):
            e.wait_ge(ldw8.sem, 16 * 4)
            for c in range(TCH):
                sl = c % 2
                e.wait_ge(ld8[sl].sem, 64 * (c // 2 + 1))
                e.wait_ge(cVE8.sem, 6 * c + 1)
                if c > 0:
                    e.wait_ge(cVE8.sem, 6 * (c - 1) + 2)
                for f in range(4):
                    cPE8.inc(e.transpose(pbank[f][:, 0:128],
                                         onrm[:, f * 128:(f + 1) * 128],
                                         ident[:]))
                e.wait_ge(cVE8.sem, 6 * c + 2)
                if c > 0:
                    e.wait_ge(cVE8.sem, 6 * (c - 1) + 3)
                for hh in range(2):
                    b = pbank[4 + hh]
                    for f in range(4):
                        mm = e.matmul(b[:, :], ofm[:, f * 128:(f + 1) * 128],
                                      wout_sb[f][:, hh * 512:(hh + 1) * 512],
                                      start=(f == 0), stop=(f == 3))
                    cPE8.inc(mm)

        @blk.vector
        def _(e):
            for c in range(TCH):
                sl = c % 2
                e.wait_ge(ld8[sl].sem, 64 * (c // 2 + 1))
                o4 = oin[sl][:, :].rearrange("p (q h j) -> p q h j", q=4, h=2, j=65)
                e.reciprocal(stat[:, 16:24].rearrange("p (a b) -> p a b", a=4),
                             o4[:, :, :, 64])
                e.drain()
                e.tensor_tensor(
                    onrm[:, :].rearrange("p (q h j) -> p q h j", q=4, h=2, j=64),
                    o4[:, :, :, 0:64],
                    stat[:, 16:24].rearrange("p (a b) -> p a b", a=4)
                    .rearrange("p a b -> p a b 1" if False else "p (a b) -> p a b",
                               a=4)[:, :, :, None] if False else
                    stat[:, 16:24].rearrange("p (q h) -> p q h", q=4)
                    .broadcast_to((128, 4, 2, 64)),
                    op=Alu.mult)
                e.drain()
                cVE8.mark(e)
                for f in range(4):
                    e.wait_ge(cPE8.sem, 6 * c + f + 1)
                    e.tensor_copy(ofm[:, f * 128:(f + 1) * 128], pbank[f][:, 0:128])
                e.drain()
                cVE8.mark(e)
                e.wait_ge(cPE8.sem, 6 * c + 6)
                for hh in range(2):
                    e.tensor_tensor(t0[:, hh * 512:(hh + 1) * 512],
                                    pbank[4 + hh][:, :],
                                    bc_bout[:, hh * 512:(hh + 1) * 512], op=Alu.add)
                e.drain()
                e.tensor_reduce(stat[:, 0:1], t0[:, :], axis=AX.X, op=Alu.add)
                e.drain()
                e.tensor_scalar(stat[:, 2:3], stat[:, 0:1], 1.0 / D, None, Alu.mult)
                e.drain()
                e.scalar_tensor_tensor(stat[:, 4:5], stat[:, 2:3], -1.0,
                                       stat[:, 2:3], op0=Alu.mult, op1=Alu.mult)
                e.drain()
                cVE8.mark(e)
                e.wait_ge(cAC8.sem, 3 * c + 1)
                e.tensor_tensor(stat[:, 5:6], stat[:, 3:4], stat[:, 4:5],
                                op=Alu.add)
                e.drain()
                e.tensor_scalar(stat[:, 13:14], stat[:, 5:6], LN_EPS, None, Alu.add)
                e.drain()
                e.reciprocal(stat[:, 14:15], stat[:, 13:14])
                e.drain()
                cVE8.mark(e)
                e.wait_ge(cAC8.sem, 3 * c + 2)
                e.scalar_tensor_tensor(stat[:, 7:8], stat[:, 2:3], -1.0,
                                       stat[:, 6:7], op0=Alu.mult, op1=Alu.mult)
                e.drain()
                cVE8.mark(e)
                e.wait_ge(cAC8.sem, 3 * c + 3)
                e.tensor_tensor(z0[:, :], z0[:, :], bc_g[:, :], op=Alu.mult)
                e.drain()
                e.tensor_tensor(z0[:, :], z0[:, :], bc_b[:, :], op=Alu.add)
                e.drain()
                e.tensor_tensor(z0[:, :], z0[:, :], xn0in[sl][:, :], op=Alu.add)
                e.drain()
                e.scalar_tensor_tensor(z0[:, :], z0[:, :], RSQRT2, g0in[sl][:, :],
                                       op0=Alu.mult, op1=Alu.mult)
                e.drain()
                if c >= 2:
                    e.wait_ge(sta8.sem, 16 * c)
                e.tensor_tensor(x1o[sl][:, :], z0[:, :], x0in[sl][:, :], op=Alu.add)
                e.drain()
                cVE8.mark(e)

        @blk.gpsimd
        def _(e):
            for c in range(TCH):
                sl = c % 2
                e.wait_ge(cVE8.sem, 6 * c + 6)
                sta8.inc(e.dma_start(out=scrX1[c * 128:(c + 1) * 128, :],
                                     in_=x1o[sl][:, :]), 16)

        @blk.scalar
        def _(e):
            for c in range(TCH):
                e.wait_ge(cVE8.sem, 6 * c + 3)
                e.activation(sqs[:, :], t0[:, :], AF.Square, accum_out=stat[:, 1:2])
                e.drain()
                e.activation(stat[:, 3:4], stat[:, 1:2], AF.Copy, bias=0.0,
                             scale=1.0 / D)
                e.drain()
                cAC8.mark(e)
                e.wait_ge(cVE8.sem, 6 * c + 4)
                e.activation(stat[:, 6:7], stat[:, 14:15], AF.Sqrt)
                e.drain()
                cAC8.mark(e)
                e.wait_ge(cVE8.sem, 6 * c + 5)
                e.activation(z0[:, :], t0[:, :], AF.Identity, bias=stat[:, 7:8],
                             scale=stat[:, 6:7])
                e.drain()
                cAC8.mark(e)

    ready["X1"] = (sta8, sta8.n)

    AR.reset()
    x1in = [AR.take(D), AR.take(D)]
    xn1t = AR.take(D)
    sq9 = AR.take(D)
    tA9 = AR.take(D)
    tB9 = AR.take(D)
    tC9 = AR.take(D)
    tst9 = AR.take(D)
    ARB.reset()
    qib = ARB.take(D)
    tst9q = ARB.take(D)
    ld9 = [fresh(), fresh()]
    cPE9 = fresh()
    cVE9 = fresh()
    cAC9 = fresh()
    stv9 = fresh()
    all_store_ctrs.append(stv9)
    stv9b = fresh()
    all_store_ctrs.append(stv9b)
    sta9 = fresh()
    all_store_ctrs.append(sta9)

    with nc.Block() as blk:
        @blk.sync
        def _(e):
            wait_stores(e)
            wait_ready(e, "X1")
            for c in range(TCH):
                sl = c % 2
                if c >= 2:
                    e.wait_ge(cAC9.sem, 3 * (c - 2) + 2)
                ld9[sl].inc(e.dma_start(out=x1in[sl][:, :],
                                        in_=scrX1[c * 128:(c + 1) * 128, :]), 16)

        @blk.vector
        def _(e):
            for c in range(TCH):
                sl = c % 2
                e.wait_ge(ld9[sl].sem, 16 * (c // 2 + 1))
                xt = x1in[sl]
                e.tensor_reduce(stat[:, 0:1], xt[:, :], axis=AX.X, op=Alu.add)
                e.tensor_tensor(sq9[:, :], xt[:, :], xt[:, :], op=Alu.mult)
                e.drain()
                e.tensor_reduce(stat[:, 1:2], sq9[:, :], axis=AX.X, op=Alu.add)
                e.drain()
                e.tensor_scalar(stat[:, 2:3], stat[:, 0:1], 1.0 / D, None, Alu.mult)
                e.tensor_scalar(stat[:, 3:4], stat[:, 1:2], 1.0 / D, None, Alu.mult)
                e.drain()
                e.scalar_tensor_tensor(stat[:, 4:5], stat[:, 2:3], -1.0,
                                       stat[:, 2:3], op0=Alu.mult, op1=Alu.mult)
                e.drain()
                e.tensor_tensor(stat[:, 5:6], stat[:, 3:4], stat[:, 4:5], op=Alu.add)
                e.drain()
                e.tensor_scalar(stat[:, 13:14], stat[:, 5:6], LN_EPS, None, Alu.add)
                e.drain()
                e.reciprocal(stat[:, 14:15], stat[:, 13:14])
                e.drain()
                cVE9.mark(e)
                e.wait_ge(cAC9.sem, 3 * c + 1)
                e.scalar_tensor_tensor(stat[:, 7:8], stat[:, 2:3], -1.0,
                                       stat[:, 6:7], op0=Alu.mult, op1=Alu.mult)
                e.drain()
                cVE9.mark(e)
                e.wait_ge(cAC9.sem, 3 * c + 3)
                e.tensor_reduce(stat[:, 9:10], xn1t[:, :], axis=AX.X, op=Alu.max,
                                apply_absolute_value=True)
                e.drain()
                e.tensor_scalar(stat[:, 10:11], stat[:, 9:10], 1e-12,
                                float(np.float32(1.0) / np.float32(448.0)),
                                Alu.max, Alu.mult)
                e.drain()
                e.tensor_copy(xs_all[:, c:c + 1], stat[:, 10:11])
                e.reciprocal(stat[:, 11:12], stat[:, 10:11])
                e.drain()
                e.tensor_scalar(tA9[:, :], xn1t[:, :], stat[:, 11:12], None,
                                Alu.mult)
                e.drain()
                e.scalar_tensor_tensor(tB9[:, :], tA9[:, :], -1.0, tA9[:, :],
                                       op0=Alu.mult, op1=Alu.max)
                e.drain()
                e.tensor_scalar(tC9[:, :], tB9[:, :], float(2.0 ** -6),
                                float(2.0 ** -6), Alu.is_lt, Alu.mult)
                e.drain()
                e.tensor_tensor(tB9[:, :], tB9[:, :], tC9[:, :], op=Alu.add)
                e.drain()
                e.tensor_copy(u32a[:, :], tB9[:, :].bitcast(U32))
                e.drain()
                e.tensor_scalar(u32a[:, :], u32a[:, :], 0xFFF00000, None,
                                Alu.bitwise_and)
                e.drain()
                e.tensor_scalar(u32b[:, :], u32a[:, :], 23, None,
                                Alu.logical_shift_right)
                e.drain()
                e.tensor_scalar(u32b[:, :], u32b[:, :], 3.0, 8388608.0,
                                Alu.subtract, Alu.mult)
                e.drain()
                e.scalar_tensor_tensor(tA9[:, :], u32b[:, :].bitcast(F32), 0.5,
                                       u32a[:, :].bitcast(F32),
                                       op0=Alu.mult, op1=Alu.add)
                e.drain()
                e.tensor_tensor(tA9[:, :], tB9[:, :], tA9[:, :], op=Alu.is_ge)
                e.drain()
                e.tensor_tensor(tA9[:, :], tA9[:, :], u32b[:, :].bitcast(F32),
                                op=Alu.mult)
                e.drain()
                e.tensor_tensor(tA9[:, :], tA9[:, :], u32a[:, :].bitcast(F32),
                                op=Alu.add)
                e.drain()
                e.tensor_tensor(tA9[:, :], tA9[:, :], tC9[:, :], op=Alu.subtract)
                e.drain()
                e.tensor_tensor(qib[:, :], tA9[:, :], sq9[:, :], op=Alu.mult)
                e.drain()
                cVE9.mark(e)
                if c > 0:
                    e.wait_ge(stv9.sem, 16 * c)
                for f in range(DCH):
                    e.wait_ge(cPE9.sem, 16 * c + f + 1)
                    e.tensor_copy(tst9[:, f * 128:(f + 1) * 128],
                                  pbank[f][:, 0:128])
                e.drain()
                cVE9.mark(e)
                if c > 0:
                    e.wait_ge(stv9b.sem, 16 * c)
                for f in range(DCH):
                    e.wait_ge(cPE9.sem, 16 * c + 8 + f + 1)
                    e.tensor_copy(tst9q[:, f * 128:(f + 1) * 128],
                                  pbank[f][:, 0:64].bitcast(BF16))
                e.drain()
                cVE9.mark(e)

        @blk.gpsimd
        def _(e):
            for c in range(TCH):
                e.wait_ge(cVE9.sem, 5 * c + 4)
                stv9.inc(e.dma_start(
                    out=scrXN1fm[:, c * 128:(c + 1) * 128].rearrange(
                        "(f p) t -> p f t", p=128),
                    in_=tst9[:, :].rearrange("p (f t) -> p f t", t=128)), 16)
                e.wait_ge(cVE9.sem, 5 * c + 5)
                stv9b.inc(e.dma_start(
                    out=scrQI[c, :, :].rearrange("(f p) t -> p f t", p=128),
                    in_=tst9q[:, :].rearrange("p (f t) -> p f t", t=128)), 16)

        @blk.scalar
        def _(e):
            for c in range(TCH):
                sl = c % 2
                e.wait_ge(cVE9.sem, 5 * c + 1)
                e.activation(stat[:, 6:7], stat[:, 14:15], AF.Sqrt)
                e.drain()
                cAC9.mark(e)
                e.wait_ge(cVE9.sem, 5 * c + 2)
                if c > 0:
                    e.wait_ge(sta9.sem, 16 * c)
                e.activation(xn1t[:, :], x1in[sl][:, :], AF.Identity,
                             bias=stat[:, 7:8], scale=stat[:, 6:7])
                e.drain()
                cAC9.mark(e)
                sta9.inc(e.dma_start(out=scrXN1tm[c * 128:(c + 1) * 128, :],
                                     in_=xn1t[:, :]), 16)
                e.activation(sq9[:, :], xn1t[:, :], AF.Sign)
                e.drain()
                cAC9.mark(e)

        @blk.tensor
        def _(e):
            for c in range(TCH):
                e.wait_ge(cAC9.sem, 3 * c + 2)
                if c > 0:
                    e.wait_ge(cVE9.sem, 5 * c)
                for f in range(DCH):
                    cPE9.inc(e.transpose(pbank[f][:, 0:128],
                                         xn1t[:, f * 128:(f + 1) * 128], ident[:]))
                e.wait_ge(cVE9.sem, 5 * c + 4)
                for f in range(DCH):
                    cPE9.inc(e.transpose(pbank[f][:, 0:64].bitcast(BF16),
                                         qib[:, f * 128:(f + 1) * 128], identb[:]))

    ready["XN1tm"] = (sta9, sta9.n)
    ready["XN1fm"] = (stv9, stv9.n)
    ready["QI"] = (stv9b, stv9b.n)

    AR.reset()
    xn1s = [AR.take(D) for _ in range(16)]
    gst = AR.take(D)
    ldA = fresh()
    cPEa = fresh()
    cVEa = fresh()
    stva = fresh()
    all_store_ctrs.append(stva)

    with nc.Block() as blk:
        @blk.sync
        def _(e):
            wait_stores(e)
            wait_ready(e, "XN1tm")
            for s in range(2):
                if s == 1:
                    e.wait_ge(cPEa.sem, 16)
                for tloc in range(16):
                    ldA.inc(e.dma_start(
                        out=xn1s[tloc][:, :],
                        in_=scrXN1tm[(s * 16 + tloc) * 128:
                                     (s * 16 + tloc + 1) * 128, :]), 16)

        @blk.tensor
        def _(e):
            for s in range(2):
                e.wait_ge(ldA.sem, 16 * 16 * (s + 1))
                for ic in range(DCH):
                    for jh in range(2):
                        grp = 16 * s + 2 * ic + jh
                        b = pbank[grp % 6]
                        if grp >= 6:
                            e.wait_ge(cVEa.sem, grp - 5)
                        for tloc in range(16):
                            mm = e.matmul(b[:, :],
                                          xn1s[tloc][:, ic * 128:(ic + 1) * 128],
                                          xn1s[tloc][:, jh * 512:(jh + 1) * 512],
                                          start=(tloc == 0), stop=(tloc == 15))
                        cPEa.inc(mm)

        @blk.vector
        def _(e):
            for s in range(2):
                for ic in range(DCH):
                    g8 = 8 * s + ic
                    if g8 >= 1:
                        e.wait_ge(stva.sem, 16 * g8)
                    for jh in range(2):
                        b = pbank[(16 * s + 2 * ic + jh) % 6]
                        e.wait_ge(cPEa.sem, 16 * s + 2 * ic + jh + 1)
                        e.tensor_copy(gst[:, jh * 512:(jh + 1) * 512], b[:, :])
                        e.drain()
                        cVEa.mark(e)

        @blk.gpsimd
        def _(e):
            for s in range(2):
                for ic in range(DCH):
                    e.wait_ge(cVEa.sem, 16 * s + 2 * ic + 2)
                    stva.inc(e.dma_start(
                        out=scrG1[(s * DCH + ic) * 128:(s * DCH + ic + 1) * 128, :],
                        in_=gst[:, :]), 16)

    ready["G1"] = (stva, stva.n)

    AR.reset()
    wqt_sb = [AR.take(D) for _ in range(DCH)]
    xnf9 = [AR.take(DCH * 512), AR.take(DCH * 512)]
    qst9 = AR.take(512)
    ldwb = fresh()
    ldxb = [fresh(), fresh()]
    cPEb = fresh()
    cVEb = fresh()
    stvb = fresh()
    all_store_ctrs.append(stvb)

    with nc.Block() as blk:
        @blk.sync
        def _(e):
            wait_stores(e)
            wait_ready(e, "WQT", "XN1fm")
            for f in range(DCH):
                ldwb.inc(e.dma_start(out=wqt_sb[f][:, :],
                                     in_=scrWQT[f * 128:(f + 1) * 128, :]), 16)
            for ts in range(8):
                sl = ts % 2
                if ts >= 2:
                    e.wait_ge(cPEb.sem, 8 * (ts - 1))
                ldxb[sl].inc(e.dma_start(
                    out=xnf9[sl][:, :],
                    in_=scrXN1fm[:, ts * 512:(ts + 1) * 512].rearrange(
                        "(f p) t -> p f t", p=128)), 16)

        @blk.tensor
        def _(e):
            e.wait_ge(ldwb.sem, 16 * DCH)
            for ts in range(8):
                sl = ts % 2
                e.wait_ge(ldxb[sl].sem, 16 * (ts // 2 + 1))
                for oc in range(DCH):
                    b = pbank[oc % 4]
                    grp = 8 * ts + oc
                    if grp >= 4:
                        e.wait_ge(cVEb.sem, grp - 3)
                    for f in range(DCH):
                        mm = e.matmul(b[:, :],
                                      wqt_sb[f][:, oc * 128:(oc + 1) * 128],
                                      xnf9[sl][:, f * 512:(f + 1) * 512],
                                      start=(f == 0), stop=(f == DCH - 1))
                    cPEb.inc(mm)

        @blk.vector
        def _(e):
            for ts in range(8):
                for oc in range(DCH):
                    b = pbank[oc % 4]
                    g9 = 8 * ts + oc
                    if g9 >= 1:
                        e.wait_ge(stvb.sem, 64 * g9)
                    e.wait_ge(cPEb.sem, 8 * ts + oc + 1)
                    e.tensor_copy(qst9[:, :], b[:, :])
                    e.drain()
                    cVEb.mark(e)

        @blk.gpsimd
        def _(e):
            for ts in range(8):
                for oc in range(DCH):
                    e.wait_ge(cVEb.sem, 8 * ts + oc + 1)
                    for tc in range(4):
                        stvb.inc(e.dma_start(
                            out=scrQ1b[4 * ts + tc, oc * 128:(oc + 1) * 128, :],
                            in_=qst9[:, tc * 128:(tc + 1) * 128]), 16)

    ready["Q1"] = (stvb, stvb.n)

    AR.reset()
    gq = [AR.take(D) for _ in range(DCH)]
    q1in = [AR.take(D), AR.take(D)]
    x1in0 = [AR.take(D), AR.take(D)]
    g1pre = AR.take(D)
    g1t = AR.take(D)
    oout = [AR.take(D), AR.take(D)]
    ARB.reset()
    qiin = [AR.take(512).bitcast(BF16), AR.take(512).bitcast(BF16)]
    w1tin = [ARB.take(D) for _ in range(DCH)]
    ldc = [fresh(), fresh()]
    ldg = fresh()
    ldwc = fresh()
    cPEc = fresh()
    cVEc = fresh()
    cACc = fresh()
    stac = fresh()
    all_store_ctrs.append(stac)

    with nc.Block() as blk:
        @blk.sync
        def _(e):
            wait_stores(e)
            wait_ready(e, "G1", "Q1", "QI", "W1T", "X1")
            for f in range(DCH):
                ldwc.inc(e.dma_start(out=w1tin[f][:, :],
                                     in_=scrW1T[f * 128:(f + 1) * 128, :]), 16)
            for s in range(2):
                if s == 1:
                    e.wait_ge(cPEc.sem, 4 * 16)
                for f in range(DCH):
                    ldg.inc(e.dma_start(
                        out=gq[f][:, :],
                        in_=scrG1[(s * DCH + f) * 128:(s * DCH + f + 1) * 128, :]),
                        16)
                for cl in range(16):
                    c = s * 16 + cl
                    sl = c % 2
                    if c >= 2:
                        e.wait_ge(cVEc.sem, 2 * (c - 1))
                    ldc[sl].inc(e.dma_start(
                        out=q1in[sl][:, :],
                        in_=scrQ1b[c, :, :].rearrange("(f p) t -> p f t", p=128)),
                        16)
                    ldc[sl].inc(e.dma_start(
                        out=qiin[sl][:, :],
                        in_=scrQI[c, :, :].rearrange("(f p) t -> p f t", p=128)),
                        16)
                    ldc[sl].inc(e.dma_start(
                        out=x1in0[sl][:, :],
                        in_=scrX1[c * 128:(c + 1) * 128, :]), 16)

        @blk.tensor
        def _(e):
            e.wait_ge(ldwc.sem, 16 * DCH)
            for s in range(2):
                e.wait_ge(ldg.sem, 16 * DCH * (s + 1))
                for cl in range(16):
                    c = s * 16 + cl
                    sl = c % 2
                    e.wait_ge(ldc[sl].sem, 48 * (c // 2 + 1))
                    if c > 0:
                        e.wait_ge(cVEc.sem, 2 * c)
                    for hh in range(2):
                        b = pbank[hh]
                        for f in range(DCH):
                            mm = e.matmul(b[:, :],
                                          q1in[sl][:, f * 128:(f + 1) * 128],
                                          gq[f][:, hh * 512:(hh + 1) * 512],
                                          start=(f == 0), stop=(f == DCH - 1))
                        cPEc.inc(mm)
                    for hh in range(2):
                        b = pbank[2 + hh]
                        for f in range(DCH):
                            mm = e.matmul(b[:, :],
                                          qiin[sl][:, f * 128:(f + 1) * 128],
                                          w1tin[f][:, hh * 512:(hh + 1) * 512],
                                          start=(f == 0), stop=(f == DCH - 1))
                        cPEc.inc(mm)

        @blk.vector
        def _(e):
            e.wait_ge(dl0.sem, dl0.n)
            for c in range(TCH):
                sl = c % 2
                e.wait_ge(cPEc.sem, 4 * c + 4)
                e.tensor_tensor(stat[:, 12:13], xs_all[:, c:c + 1], ws_sb[:, :],
                                op=Alu.mult)
                e.drain()
                for hh in range(2):
                    e.scalar_tensor_tensor(g1pre[:, hh * 512:(hh + 1) * 512],
                                           pbank[2 + hh][:, :], stat[:, 12:13],
                                           bc_b1[:, hh * 512:(hh + 1) * 512],
                                           op0=Alu.mult, op1=Alu.add)
                e.drain()
                cVEc.mark(e)
                e.wait_ge(cACc.sem, c + 1)
                if c >= 2:
                    e.wait_ge(stac.sem, 16 * c)
                for hh in range(2):
                    e.scalar_tensor_tensor(oout[sl][:, hh * 512:(hh + 1) * 512],
                                           pbank[hh][:, :], 1.0,
                                           g1t[:, hh * 512:(hh + 1) * 512],
                                           op0=Alu.mult, op1=Alu.mult)
                e.drain()
                e.tensor_tensor(oout[sl][:, :], oout[sl][:, :], x1in0[sl][:, :],
                                op=Alu.add)
                e.drain()
                cVEc.mark(e)

        @blk.gpsimd
        def _(e):
            for c in range(TCH):
                sl = c % 2
                e.wait_ge(cVEc.sem, 2 * c + 2)
                stac.inc(e.dma_start(out=out_h[c * 128:(c + 1) * 128, :],
                                     in_=oout[sl][:, :]), 16)

        @blk.scalar
        def _(e):
            for c in range(TCH):
                e.wait_ge(cVEc.sem, 2 * c + 1)
                e.activation(g1t[:, :], g1pre[:, :], AF.Silu)
                e.drain()
                cACc.mark(e)

    with nc.Block() as blk:
        @blk.sync
        def _(e):
            wait_stores(e)
            e.wait_ge(stac.sem, stac.n)

    es.close()
    return nc


_CACHE = {}


def kernel(**inputs):
    if "nc" not in _CACHE:
        _CACHE["nc"] = build_program()
    nc = _CACHE["nc"]
    x = np.ascontiguousarray(np.asarray(inputs["L0_attn_x"], dtype=np.float32))
    w18 = np.asarray(inputs["L1_ffn_weight"]).view(np.uint8)
    wsc = np.asarray(inputs["L1_ffn_weight_scale"], dtype=np.float32).reshape(1, 1)
    common = {
        "Wqk": np.ascontiguousarray(np.asarray(inputs["L0_attn_W_qk"], np.float32)),
        "bqk": np.ascontiguousarray(np.asarray(inputs["L0_attn_b_qk"], np.float32)),
        "Wv": np.ascontiguousarray(np.asarray(inputs["L0_attn_W_v"], np.float32)),
        "bv": np.ascontiguousarray(np.asarray(inputs["L0_attn_b_v"], np.float32)),
        "Wout": np.ascontiguousarray(np.asarray(inputs["L0_attn_W_out"], np.float32)),
        "bout": np.ascontiguousarray(np.asarray(inputs["L0_attn_b_out"], np.float32)),
        "lng": np.ascontiguousarray(np.asarray(inputs["L0_attn_ln_gamma"], np.float32)),
        "lnb": np.ascontiguousarray(np.asarray(inputs["L0_attn_ln_beta"], np.float32)),
        "Wg": np.ascontiguousarray(np.asarray(inputs["L0_ffn_w_gate"], np.float32)),
        "Wu": np.ascontiguousarray(np.asarray(inputs["L0_ffn_w_up"], np.float32)),
        "Wd": np.ascontiguousarray(np.asarray(inputs["L0_ffn_w_down"], np.float32)),
        "Wq1": np.ascontiguousarray(np.asarray(inputs["L1_attn_W_q"], np.float32)),
        "W18": np.ascontiguousarray(w18),
        "Wsc": wsc,
        "b1v": np.ascontiguousarray(np.asarray(inputs["L1_ffn_bias"], np.float32)),
    }
    nper = B_FULL // NCORES
    in_maps = []
    for ci in range(NCORES):
        m = dict(common)
        m["x_in"] = np.ascontiguousarray(
            x[ci * nper:(ci + 1) * nper].reshape(T, D))
        in_maps.append(m)
    res = run_bass_kernel_spmd(nc, in_maps, core_ids=list(range(NCORES)))
    outs = [np.asarray(r["out"]).reshape(nper, S, D) for r in res.results]
    return np.concatenate(outs, axis=0)
